# version 09
# baseline (speedup 1.0000x reference)
import os
import numpy as np
import concourse.bass as bass
import concourse.mybir as mybir
from concourse.bass_utils import run_bass_kernel_spmd

F32 = mybir.dt.float32
BF16 = mybir.dt.bfloat16
AF = mybir.ActivationFunctionType
ALU = mybir.AluOpType

NCORES = 8
B, N, D = 4, 8192, 1024
H, DH = 16, 64
TPC = 4096
SEG = 512
NSEG = TPC // SEG
CH = 256
CPS = SEG // CH
NCH = TPC // CH
NT = SEG // 128
NJ = D // 128
GAP = 4
EPS1 = 1e-5
EPS2 = 1e-6

_SET = {AF.Exp: "nle", AF.Ln: "nle", AF.Sigmoid: "sig", AF.Silu: "silu"}
ENGS = ("sync", "tensor", "vector", "scalar", "gpsimd")


class _Sem:
    __slots__ = ("name", "h", "count")

    def __init__(self, name):
        self.name, self.h, self.count = name, None, 0


class KB:

    def __init__(self, nc):
        self.nc = nc
        self.steps = {e: [] for e in ENGS}
        self.sems = []
        self.cur_set = None

    def sem(self, name):
        s = _Sem(name)
        self.sems.append(s)
        return s

    def step(self, eng, emit, waits=(), inc=None, inc_n=1, fence=False):
        w = []
        for x in waits:
            if isinstance(x, _Sem):
                if x.count > 0:
                    w.append((x, x.count))
            elif x is not None:
                s, v = x
                if v > 0:
                    w.append((s, v))
        if inc is not None:
            inc.count += inc_n
        self.steps[eng].append((emit, tuple(w), inc, inc_n, fence))
        return (inc, inc.count) if inc is not None else None

    def dma(self, eng, dst, src, sem, waits=(), slow=False):
        if slow:
            return self.step(eng, lambda e, d=dst, s=src: e.dma_start(
                out=d, in_=s, allow_slow_non_contiguous=True),
                waits=waits, inc=sem, inc_n=16)
        return self.step(eng, lambda e, d=dst, s=src: e.dma_start(
            out=d, in_=s), waits=waits, inc=sem, inc_n=16)

    def act(self, emitf, func, waits=(), inc=None):
        new_set = _SET.get(func)
        fence = new_set is not None and self.cur_set is not None \
            and new_set != self.cur_set
        if new_set is not None:
            self.cur_set = new_set

        def emit(e, f=emitf, nf=fence):
            if nf:
                e.drain()
            return f(e)
        return self.step("scalar", emit, waits=waits, inc=inc)

    def vec(self, emitf, waits=(), inc=None, fence=False):
        return self.step("vector", emitf, waits=waits, inc=inc, fence=fence)

    def emit(self, ctx):
        for s in self.sems:
            s.h = ctx.enter_context(self.nc.semaphore(s.name))
        blk = ctx.enter_context(self.nc.Block())
        for ename in ENGS:
            steps = self.steps[ename]

            def body(eng, _steps=steps, _e=ename):
                last = {}
                for emit, waits, inc, inc_n, fence in _steps:
                    for s, v in waits:
                        if last.get(s, 0) < v:
                            eng.wait_ge(s.h, v)
                            last[s] = v
                    ins = emit(eng)
                    if _e == "vector":
                        if inc is not None:
                            if ins is None:
                                ins = eng.nop()
                            ins.then_inc(inc.h, inc_n)
                        if fence:
                            eng.drain()
                        eng.nop()
                    elif inc is not None:
                        if ins is None:
                            ins = eng.nop()
                        ins.then_inc(inc.h, inc_n)
            getattr(blk, ename)(body)



def _host_build(inputs):
    x = np.ascontiguousarray(np.asarray(inputs["L0_attn_x"]), dtype=np.float32)
    ipw = np.asarray(inputs["L0_attn_in_proj_w"], dtype=np.float32)
    dtb = np.asarray(inputs["L0_attn_dt_bias"], dtype=np.float32)
    convw = np.asarray(inputs["L0_attn_conv_w"], dtype=np.float32)
    convb = np.asarray(inputs["L0_attn_conv_b"], dtype=np.float32)
    nsc = np.asarray(inputs["L0_attn_norm_scale"], dtype=np.float32)
    hm = np.asarray(inputs["L0_attn_head_mix"], dtype=np.float32)
    opw = np.asarray(inputs["L0_attn_out_proj_w"], dtype=np.float32)
    wq = np.asarray(inputs["L0_ffn_Wq"], dtype=np.float32)
    rt = np.asarray(inputs["L0_ffn_routers"], dtype=np.float32)
    lv = np.asarray(inputs["L0_ffn_leaf_values"], dtype=np.float32)
    wo = np.asarray(inputs["L0_ffn_Wout_w"], dtype=np.float32)
    wob = np.asarray(inputs["L0_ffn_Wout_b"], dtype=np.float32)

    p = np.arange(D)
    old = (p % 16) * 64 + (p // 16)

    ipw_p = np.ascontiguousarray(np.concatenate(
        [ipw[:, t * D:(t + 1) * D][:, old] for t in range(3)], axis=1))
    dtb_p = np.ascontiguousarray(dtb[old].reshape(D, 1))
    opw_p = np.ascontiguousarray(opw[old, :])

    convd = np.zeros((4, NJ, 128, 128), dtype=np.float32)
    for kk in range(4):
        for j in range(NJ):
            convd[kk, j][np.arange(128), np.arange(128)] = \
                convw[j * 128:(j + 1) * 128, 0, kk]
    convb_r = np.ascontiguousarray(convb.reshape(NJ, 1, 128))
    nsc_c = np.ascontiguousarray(nsc.reshape(D, 1))

    hmbd = np.zeros((128, 128), dtype=np.float32)
    for kk in range(8):
        hmbd[kk * 16:(kk + 1) * 16, kk * 16:(kk + 1) * 16] = hm

    rtT = np.zeros((D, 256), dtype=np.float32)
    rtT[:, :255] = rt.T
    lvT = np.ascontiguousarray(lv.T)
    woT = np.ascontiguousarray(wo.T)
    wob_r = np.ascontiguousarray(wob.reshape(1, D))

    sel = np.zeros((2, 4, 128, 128), dtype=np.float32)
    for leaf in range(256):
        lc, lcol = leaf // 128, leaf % 128
        for lev in range(8):
            node = (1 << lev) - 1 + (leaf >> (8 - lev))
            bit = (leaf >> (7 - lev)) & 1
            chunk = (node // 128) if bit == 0 else 2 + (node // 128)
            sel[lc, chunk, node % 128, lcol] = -1.0

    ropesel = np.zeros((4, 32, 128), dtype=np.float32)
    for j in range(4):
        for pp in range(128):
            ropesel[j, 8 * j + pp // 16, pp] = 1.0

    inv_freq = (1.0 / (10000.0 ** (np.arange(0, DH, 2, dtype=np.float32)
                                   / DH))).astype(np.float32)

    shared = dict(
        ipw=ipw_p, dtb=dtb_p, convd=convd, convb=convb_r, nsc=nsc_c,
        hmbd=hmbd, opw=opw_p, wq=np.ascontiguousarray(wq), rtT=rtT, lvT=lvT,
        woT=woT, wob=wob_r, sel=np.ascontiguousarray(sel.reshape(8, 128, 128)),
        ident=np.eye(128, dtype=np.float32),
        ropesel=np.ascontiguousarray(ropesel))

    in_maps = []
    for c in range(NCORES):
        seq, half = c // 2, c % 2
        t0 = half * TPC
        xe = np.zeros((3 + TPC, D), dtype=np.float32)
        if half == 1:
            xe[0:3] = x[seq, t0 - 3:t0]
        xe[3:] = x[seq, t0:t0 + TPC]

        tpos = np.arange(t0, t0 + TPC, dtype=np.float32)
        ang = tpos[None, :] * inv_freq[:, None]
        m = dict(shared)
        msk24 = np.zeros((128, 24), dtype=np.float32)
        if half == 1:
            for cc in range(3):
                msk24[:, cc * 8 + (c - 1)] = 1.0
        m.update(xe=np.ascontiguousarray(xe),
                 ropec=np.ascontiguousarray(np.cos(ang).astype(np.float32)),
                 ropes=np.ascontiguousarray(np.sin(ang).astype(np.float32)),
                 msk24=msk24,
                 evenflag=np.full((128, 1), 1.0 - half, dtype=np.float32))
        in_maps.append(m)
    return in_maps



def _build_program():
    nc = bass.Bass("TRN2", target_bir_lowering=False)

    def A(name, shape, dt=F32, kind="ExternalInput"):
        return nc.dram_tensor(name, list(shape), dt, kind=kind)

    d_xe = A("xe", (3 + TPC, D))
    d_ipw = A("ipw", (D, 3 * D))
    d_dtb = A("dtb", (D, 1))
    d_convd = A("convd", (4, NJ, 128, 128))
    d_convb = A("convb", (NJ, 1, 128))
    d_nsc = A("nsc", (D, 1))
    d_hmbd = A("hmbd", (128, 128))
    d_opw = A("opw", (D, D))
    d_wq = A("wq", (D, D))
    d_rtT = A("rtT", (D, 256))
    d_lvT = A("lvT", (D, 256))
    d_woT = A("woT", (D, D))
    d_wob = A("wob", (1, D))
    d_sel = A("sel", (8, 128, 128))
    d_ident = A("ident", (128, 128))
    d_ropec = A("ropec", (32, TPC))
    d_ropes = A("ropes", (32, TPC))
    d_ropesel = A("ropesel", (4, 32, 128))
    d_msk24 = A("msk24", (128, 24))
    d_even = A("evenflag", (128, 1))
    d_y = A("y", (TPC, D), F32, "ExternalOutput")
    dbg = os.environ.get("KDBG", "0") == "1"
    if dbg:
        _dbg_specs = [
            ("xn2", (128, 520), BF16), ("xc", (128, SEG), BF16),
            ("dt", (128, SEG), BF16), ("v", (128, SEG), BF16),
            ("g", (128, SEG), BF16), ("vin", (128, SEG), BF16),
            ("asp", (128, SEG), BF16), ("arows", (128, NT * NSEG), F32),
            ("s2rows", (128, NT * NSEG), F32), ("bdec", (128, NCH), F32),
            ("bout", (128, NCH), F32), ("carry", (128, NCH), F32),
            ("pm0", (128, SEG), BF16), ("pm2", (128, SEG), BF16),
            ("lp", (128, SEG), BF16), ("fin", (128, SEG), BF16),
            ("hmg", (128, SEG), BF16), ("attn", (128, SEG), BF16),
            ("stm", (128, D), F32), ("cos", (128, SEG), BF16),
            ("vmixd", (128, SEG), BF16), ("g2", (128, SEG), BF16),
            ("gat", (128, 24), F32), ("part", (128, 3), F32),
            ("contrib", (128, 3), F32), ("ucum", (128, NCH), F32),
            ("ncd", (128, NCH), F32)]
        d_dbg = {nm: A("dbg_" + nm, shp, dt, "ExternalOutput")
                 for nm, shp, dt in _dbg_specs}

        def dump(nm, src_ap, eng_waits):
            kb.dma("gpsimd", d_dbg[nm].ap(), src_ap, dq_g, waits=eng_waits)

    d_sp_g = nc.dram_tensor("sp_g", [NJ, NSEG, 128, SEG], BF16, kind="Internal")
    d_sp_v = nc.dram_tensor("sp_v", [NJ, NSEG, 128, SEG], BF16, kind="Internal")
    d_sp_a = nc.dram_tensor("sp_a", [NJ, NSEG, 128, SEG], BF16, kind="Internal")
    d_ccin = nc.dram_tensor("ccin", [NJ, 128, 3], F32, kind="Internal")
    d_ccout = nc.dram_tensor("ccout", [NCORES * NJ, 128, 3], F32,
                             kind="Internal")
    d_halo = nc.dram_tensor("halo", [3, D], BF16, kind="Internal")
    d_rTs = nc.dram_tensor("rTs", [NJ, 128, 256], BF16, kind="Internal")
    d_lv2s = nc.dram_tensor("lv2s", [2, 128, D], BF16, kind="Internal")

    def S(name, shape, dt):
        return nc.alloc_sbuf_tensor(name, list(shape), dt)

    x_tm = [S(f"x_tm{t}", (128, D), F32) for t in range(NT)]
    S_tm = [S(f"S_tm{t}", (128, D), F32) for t in range(NT)]
    xnb_tm = [S(f"xnb_tm{t}", (128, D), BF16) for t in range(NT)]
    xn2_fm = [S(f"xn2_fm{j}", (128, 520), BF16) for j in range(NJ)]
    xcf = [S(f"xcf{j}", (128, SEG), BF16) for j in range(NJ)]
    v_in = [S(f"v_in{j}", (128, SEG), BF16) for j in range(NJ)]
    a_sp = [S(f"a_sp{j}", (128, SEG), BF16) for j in range(NJ)]
    g_sp = [S(f"g_sp{j}", (128, SEG), BF16) for j in range(NJ)]
    dt_b = [S(f"dt_b{j}", (128, SEG), BF16) for j in range(NJ)]
    v_b = [S(f"v_b{j}", (128, SEG), BF16) for j in range(NJ)]
    cos_fm = [S(f"cos_fm{j}", (128, SEG), BF16) for j in range(4)]
    sin_fm = [S(f"sin_fm{j}", (128, SEG), BF16) for j in range(4)]
    rope_t = [S(f"rope_t{i}", (128, SEG), BF16) for i in range(3)]
    stg = [S(f"stg{i}", (128, 768), F32) for i in range(2)]
    small = S("small_stg", (128, 128), F32)
    spb = rope_t[2]

    xn_fm = [xn2_fm[j][:, 4:516] for j in range(NJ)]
    pm_b = cos_fm
    lp_b = sin_fm
    exp_t = stg[0][:, 0:512]
    cstg = [stg[1][0:32, 0:512], None]
    halo_x = S_tm[0][0:3, :]
    halo_dum = S_tm[1][0:3, :]
    halo_tm = xnb_tm[0][0:3, :]

    def S_at(base_th, byte_off, name, shape, dt):
        addr = nc.lookup_mloc(base_th).addr + byte_off
        return nc.alloc_sbuf_tensor_at(name, list(shape), dt, offset=addr)

    csl_b = [S_at(stg[1], 2048, "csl_b0", (32, SEG), BF16),
             None]
    cstg[1] = S_at(xnb_tm[3], 0, "cstg1_f32", (32, 512), F32)[:, :]
    dt_B = [S_at(S_tm[2 + k // 4], (k % 4) * 1024, f"dt_B{k}", (128, SEG),
                 BF16) for k in range(NJ)]
    vi_B = [S_at(S_tm[k // 4], (k % 4) * 1024, f"vi_B{k}", (128, SEG), BF16)
            for k in range(NJ)]

    ipw_b = [S(f"ipw_b{k}", (128, 3 * D), BF16) for k in range(NJ)]
    opw_b = [S(f"opw_b{k}", (128, D), BF16) for k in range(NJ)]
    rT_b = [S(f"rT_b{j}", (128, 256), BF16) for j in range(NJ)]
    lv2_b = [S(f"lv2_b{l}", (128, D), BF16) for l in range(2)]
    sel_b = [S(f"sel_b{i}", (128, 128), BF16) for i in range(8)]
    diag_b = [[S(f"diag_b{k}_{j}", (128, 128), BF16) for j in range(NJ)]
              for k in range(4)]
    hmbd_b = S("hmbd_b", (128, 128), BF16)
    ident_b = S("ident_b", (128, 128), BF16)
    ropesel_b = [S(f"ropesel_b{j}", (32, 128), BF16) for j in range(4)]
    convb_b = [S(f"convb_b{j}", (1, 128), BF16) for j in range(NJ)]
    wob_b = S("wob_b", (1, D), BF16)
    dtb_s = [S(f"dtb_s{j}", (128, 1), F32) for j in range(NJ)]
    nsc_s = [S(f"nsc_s{j}", (128, 1), F32) for j in range(NJ)]
    ones_b = S("ones_b", (1, SEG), BF16)
    msk_s = S("msk_s", (128, 24), F32)
    even_s = S("even_s", (128, 1), F32)

    csl_b[1] = S("csl_b1", (32, SEG), BF16)

    wq_b = [S_at(S_tm[k // 2], (k % 2) * 2048, f"wq_b{k}", (128, D), BF16)
            for k in range(NJ)]
    woT_b = [S_at(x_tm[k // 2], (k % 2) * 2048, f"woT_b{k}", (128, D), BF16)
             for k in range(NJ)]
    rtT_b = [S_at(xnb_tm[k // 4], (k % 4) * 512, f"rtT_b{k}", (128, 256),
                  BF16) for k in range(NJ)]
    lvT_b = [S_at(xnb_tm[2 + k // 4], (k % 4) * 512, f"lvT_b{k}", (128, 256),
                  BF16) for k in range(NJ)]

    rT_addrs = [nc.lookup_mloc(t).addr for t in rT_b]
    assert all(rT_addrs[k + 1] - rT_addrs[k] == 512 for k in range(NJ - 1)), \
        "rT_b tiles not contiguous"
    Lbuf_t = S_at(lv2_b[0], 0, "lbuf_f32", (128, 512), F32)
    Lbuf = [Lbuf_t[:, 0:CH], Lbuf_t[:, CH:2 * CH]]
    Lc_p = [S_at(lv2_b[1], 0, "lcp0_f32", (128, 257), F32)[:, :],
            S_at(rT_b[0], 0, "lcp1_f32", (128, 257), F32)[:, :]]
    la_t = S_at(rT_b[0], 1056, "la_b16", (128, 512), BF16)
    la_b = [la_t[:, 0:CH], la_t[:, CH:2 * CH]]
    y0_t = S_at(rT_b[0], 2080, "y0_b16", (128, 512), BF16)
    y0_b = [y0_t[:, 0:CH], y0_t[:, CH:2 * CH]]

    a_rows = S("a_rows", (128, NT * NSEG), F32)
    nma_rows = S("nma_rows", (128, NT * NSEG), F32)
    s2_rows = S("s2_rows", (128, NT * NSEG), F32)
    nms2_rows = S("nms2_rows", (128, NT * NSEG), F32)
    sc1_rows = S("sc1_rows", (128, NT * NSEG), F32)
    sum_c = S("sum_c", (128, NT), F32)
    sq_c = S("sq_c", (128, NT), F32)
    r_m = S("r_m", (128, NT), F32)
    r_v = S("r_v", (128, NT), F32)
    r_t1 = S("r_t1", (128, NT), F32)
    r_t2 = S("r_t2", (128, NT), F32)
    bdec = [S(f"bdec{j}", (128, NCH), F32) for j in range(NJ)]
    bout = [S(f"bout{j}", (128, NCH), F32) for j in range(NJ)]
    carry = [S(f"carry{j}", (128, NCH), F32) for j in range(NJ)]
    Ucum = S("Ucum", (128, NCH), F32)
    ncd_t = S("ncd_t", (128, NCH), F32)
    em_t = S("em_t", (128, NCH), F32)
    ep_t = S("ep_t", (128, NCH), F32)
    seed_t = S("seed_t", (128, NCH), F32)
    incl_t = S("incl_t", (128, NCH), F32)
    contrib = S("contrib", (128, 3), F32)
    gat = S("gat", (128, 24), F32)
    gsel = S("gsel", (128, 24), F32)
    partner = S("partner", (128, 3), F32)
    stabu = S("stabu", (128, 1), F32)
    hsum = S("hsum", (3, 2), F32)
    hrow = S("hrow", (3, 4), F32)

    pb16 = [nc.alloc_psum_tensor(f"pt16_{i}", [128, 1024], BF16)
            for i in range(2)]
    psum = [None, None] + [nc.alloc_psum_tensor(f"pb{i}", [128, 512], F32)
                           for i in range(2, 8)]

    stage = int(os.environ.get("KSTAGE", "9"))
    kb = KB(nc)
    dq_s = kb.sem("dq_s")
    dq_g = kb.sem("dq_g")
    pe = kb.sem("pe")
    ve = kb.sem("ve")
    se = kb.sem("se")
    cc = kb.sem("cc")

    bank_free = [None] * 8

    def mm_group(bank, mms, waits=()):
        bf = bank_free[bank]
        bw = waits

        def emit(e, _mms=tuple(mms), _bf=bf, _b=bank):
            if _bf is not None:
                e.wait_ge(_bf[0].h, _bf[1])
            n = len(_mms)
            ins = None
            for i, (lt, rh) in enumerate(_mms):
                if i > 0:
                    for _ in range(GAP):
                        e.nop()
                w = rh.shape[-1]
                ins = e.matmul(psum[_b][:, 0:w], lt, rh,
                               start=(i == 0), stop=(i == n - 1))
            return ins
        kb.step("tensor", emit, waits=waits, inc=pe)

    def bank_consumed(bank, sem):
        bank_free[bank] = (sem, sem.count)

    def transpose_to(bank, col, src_ap, waits=()):
        bf = bank_free[bank]

        def emit(e, _bf=bf, _c=col, _s=src_ap, _b=bank):
            if _bf is not None:
                e.wait_ge(_bf[0].h, _bf[1])
            return e.transpose(pb16[_b][:, _c:_c + 128], _s, ident_b[:, :])
        kb.step("tensor", emit, waits=waits, inc=pe)

    cnt = [0]

    def load_cast(dst_tiles, dram_ap, cols):
        piece = 768
        for k, dst in enumerate(dst_tiles):
            for p0 in range(0, cols, piece):
                pw = min(piece, cols - p0)
                sbuf = stg[cnt[0] % 2]
                cnt[0] += 1
                kb.dma("gpsimd", sbuf[:, 0:pw],
                       dram_ap[k * 128:(k + 1) * 128, p0:p0 + pw], dq_g,
                       waits=(ve,))
                kb.vec(lambda e, d=dst, s=sbuf, a=p0, w=pw:
                       e.tensor_copy(d[:, a:a + w], s[:, 0:w]),
                       waits=(dq_g,), inc=ve)

    load_cast(ipw_b, d_ipw.ap(), 3 * D)
    load_cast(opw_b, d_opw.ap(), D)
    load_cast(wq_b, d_wq.ap(), D)
    load_cast(rtT_b, d_rtT.ap(), 256)
    load_cast(lvT_b, d_lvT.ap(), 256)
    load_cast(woT_b, d_woT.ap(), D)

    def small_cast(dst, src_ap, pdim=128, scale=None):
        wtot = src_ap.shape[-1]
        for p0 in range(0, wtot, 128):
            w = min(128, wtot - p0)
            kb.dma("gpsimd", small[0:pdim, 0:w], src_ap[:, p0:p0 + w], dq_g,
                   waits=(ve, se))
            if scale is None:
                kb.vec(lambda e, d=dst, p=pdim, w=w, a=p0:
                       e.tensor_copy(d[0:p, a:a + w], small[0:p, 0:w]),
                       waits=(dq_g,), inc=ve)
            else:
                kb.act(lambda e, d=dst, p=pdim, w=w, a=p0, sc=scale:
                       e.activation(d[0:p, a:a + w], small[0:p, 0:w],
                                    AF.Identity, scale=sc[:, :]),
                       AF.Identity, waits=(dq_g,), inc=se)

    for j in range(NJ):
        kb.dma("gpsimd", nsc_s[j][:, :],
               d_nsc.ap()[j * 128:(j + 1) * 128, :], dq_g)
        kb.dma("gpsimd", dtb_s[j][:, :],
               d_dtb.ap()[j * 128:(j + 1) * 128, :], dq_g)
    kb.dma("gpsimd", msk_s[:, :], d_msk24.ap(), dq_g)
    kb.dma("gpsimd", even_s[:, :], d_even.ap(), dq_g)

    small_cast(hmbd_b, d_hmbd.ap())
    small_cast(ident_b, d_ident.ap())
    small_cast(wob_b, d_wob.ap(), pdim=1)
    for i in range(8):
        small_cast(sel_b[i], d_sel.ap()[i])
    for j in range(4):
        small_cast(ropesel_b[j], d_ropesel.ap()[j], pdim=32)
    for j in range(NJ):
        small_cast(convb_b[j], d_convb.ap()[j], pdim=1)
    for kk in range(4):
        for j in range(NJ):
            small_cast(diag_b[kk][j], d_convd.ap()[kk, j], scale=nsc_s[j])

    kb.vec(lambda e: e.memset(ones_b[:, :], 1.0), inc=ve)

    for j in range(NJ):
        bank = 2 + j % 2
        mm_group(bank, [(wq_b[k][:, j * 128:(j + 1) * 128],
                         rtT_b[k][:, 0:256]) for k in range(NJ)],
                 waits=(ve, se))
        kb.act(lambda e, j=j, b=bank:
               e.activation(rT_b[j][:, :], psum[b][:, 0:256], AF.Identity),
               AF.Identity, waits=(pe,), inc=se)
        bank_consumed(bank, se)
    for lc in range(2):
        for hf in range(2):
            bank = 4 + (lc * 2 + hf) % 2
            mms = [(lvT_b[k][:, lc * 128:(lc + 1) * 128],
                    woT_b[k][:, hf * 512:(hf + 1) * 512]) for k in range(NJ)]
            mms.append((ones_b[0:1, 0:128],
                        wob_b[0:1, hf * 512:(hf + 1) * 512]))
            mm_group(bank, mms, waits=(ve, se))
            kb.act(lambda e, lc=lc, hf=hf, b=bank:
                   e.activation(lv2_b[lc][:, hf * 512:(hf + 1) * 512],
                                psum[b][:, 0:512], AF.Identity),
                   AF.Identity, waits=(pe,), inc=se)
            bank_consumed(bank, se)

    for j in range(NJ):
        kb.dma("gpsimd", d_rTs.ap()[j], rT_b[j][:, :], dq_g, waits=(se,))
    for l in range(2):
        kb.dma("gpsimd", d_lv2s.ap()[l], lv2_b[l][:, :], dq_g, waits=(se,))
    for i in range(2):
        kb.vec(lambda e, i=i: e.memset(Lc_p[i][:, 0:1], 0.0),
               waits=(dq_g,), inc=ve)

    if stage < 1:
        halo_on = False
    else:
        halo_on = True
    if halo_on:
     kb.dma("sync", halo_x, d_xe.ap()[0:3, :], dq_s, waits=(pe, se))
     kb.act(lambda e: e.activation(halo_dum, halo_x, AF.Identity,
                                   accum_out=hsum[:, 0:1]),
            AF.Identity, waits=(dq_s, pe), inc=se)
     kb.act(lambda e: e.activation(halo_dum, halo_x, AF.Square,
                                   accum_out=hsum[:, 1:2]),
            AF.Square, inc=se)
     kb.vec(lambda e: e.tensor_scalar_mul(hrow[:, 0:1], hsum[:, 0:1], 1.0 / D),
            waits=(se,), inc=ve, fence=True)
     kb.vec(lambda e: e.tensor_scalar_mul(hrow[:, 1:2], hsum[:, 1:2], 1.0 / D),
            inc=ve, fence=True)
     kb.vec(lambda e: e.tensor_mul(hrow[:, 2:3], hrow[:, 0:1], hrow[:, 0:1]),
            inc=ve, fence=True)
     kb.vec(lambda e: e.tensor_sub(hrow[:, 1:2], hrow[:, 1:2], hrow[:, 2:3]),
            inc=ve, fence=True)
     kb.vec(lambda e: e.tensor_scalar(hrow[:, 1:2], hrow[:, 1:2], 1.0 + EPS2,
                                      EPS1 * EPS2, ALU.mult, ALU.add),
            inc=ve, fence=True)
     kb.act(lambda e: e.activation(hrow[:, 2:3], hrow[:, 1:2], AF.Ln),
            AF.Ln, waits=(ve,), inc=se)
     kb.act(lambda e: e.activation(hrow[:, 2:3], hrow[:, 2:3], AF.Exp,
                                   scale=-0.5),
            AF.Exp, inc=se)
     kb.vec(lambda e: e.scalar_tensor_tensor(hrow[:, 3:4], hrow[:, 0:1], -1.0,
                                             hrow[:, 2:3], ALU.mult, ALU.mult),
            waits=(se,), inc=ve, fence=True)
     kb.act(lambda e: e.activation(halo_tm, halo_x, AF.Identity,
                                   scale=hrow[:, 2:3], bias=hrow[:, 3:4]),
            AF.Identity, waits=(ve,), inc=se)
     kb.dma("gpsimd", d_halo.ap(), halo_tm, dq_g, waits=(se,))
     for j in range(NJ):
         kb.dma("gpsimd", xn2_fm[j][:, 0:3],
                d_halo.ap()[:, j * 128:(j + 1) * 128].rearrange("a b -> b a"),
                dq_g, waits=(dq_g,), slow=True)

    OC_ORDER = list(range(16, 24)) + list(range(8, 16)) + list(range(0, 8))

    nseg1 = 0 if stage < 2 else (1 if stage == 2 else NSEG)
    sub = int(os.environ.get("KSUB", "9"))

    def p1_front(s):
        dtP = dt_b if s % 2 == 0 else dt_B
        viP = v_in if s % 2 == 0 else vi_B
        base = s * SEG
        for t in range(NT):
            kb.dma("sync", x_tm[t][:, :],
                   d_xe.ap()[3 + base + t * 128: 3 + base + (t + 1) * 128, :],
                   dq_s, waits=(se, ve))
        csl_toks = []
        for i, dsrc in enumerate((d_ropec, d_ropes)):
            kb.dma("sync", cstg[i], dsrc.ap()[:, base:base + SEG],
                   dq_s, waits=(se, pe))
            csl_toks.append(kb.act(lambda e, i=i: e.activation(
                csl_b[i][:, :], cstg[i], AF.Identity),
                AF.Identity, waits=(dq_s,), inc=se))
        for j in range(4):
            for i, dst in enumerate((cos_fm, sin_fm)):
                bank = 6 + ((2 * j + i) % 2)
                mm_group(bank, [(ropesel_b[j][:, :], csl_b[i][:, :])],
                         waits=(csl_toks[i],))
                kb.act(lambda e, j=j, b=bank, d=dst:
                       e.activation(d[j][:, :], psum[b][:, 0:SEG], AF.Identity),
                       AF.Identity, waits=(pe, ve), inc=se)
                bank_consumed(bank, se)

        for t in range(NT):
            kb.act(lambda e, t=t:
                   e.activation(exp_t, x_tm[t][:, 0:512], AF.Identity,
                                accum_out=sum_c[:, t:t + 1]),
                   AF.Identity, waits=(dq_s, ve), inc=se)
            kb.act(lambda e, t=t:
                   e.activation(exp_t, x_tm[t][:, 512:1024], AF.Identity,
                                accum_out=r_t1[:, t:t + 1]),
                   AF.Identity, inc=se)
            kb.act(lambda e, t=t:
                   e.activation(exp_t, x_tm[t][:, 0:512], AF.Square,
                                accum_out=sq_c[:, t:t + 1]),
                   AF.Square, inc=se)
            kb.act(lambda e, t=t:
                   e.activation(exp_t, x_tm[t][:, 512:1024], AF.Square,
                                accum_out=r_t2[:, t:t + 1]),
                   AF.Square, inc=se)
        R = slice(s * NT, (s + 1) * NT)
        kb.vec(lambda e: e.tensor_add(sum_c[:, :], sum_c[:, :], r_t1[:, :]),
               waits=(se,), inc=ve, fence=True)
        kb.vec(lambda e: e.tensor_add(sq_c[:, :], sq_c[:, :], r_t2[:, :]),
               inc=ve, fence=True)
        kb.vec(lambda e: e.tensor_scalar_mul(r_m[:, :], sum_c[:, :], 1.0 / D),
               inc=ve, fence=True)
        kb.vec(lambda e: e.tensor_scalar_mul(r_v[:, :], sq_c[:, :], 1.0 / D),
               inc=ve, fence=True)
        kb.vec(lambda e: e.tensor_mul(r_t1[:, :], r_m[:, :], r_m[:, :]),
               inc=ve, fence=True)
        kb.vec(lambda e: e.tensor_sub(r_v[:, :], r_v[:, :], r_t1[:, :]),
               inc=ve, fence=True)
        kb.vec(lambda e: e.tensor_scalar_add(r_t1[:, :], r_v[:, :], EPS1),
               inc=ve, fence=True)
        kb.act(lambda e: e.activation(r_t1[:, :], r_t1[:, :], AF.Ln),
               AF.Ln, waits=(ve,), inc=se)
        kb.act(lambda e, R=R: e.activation(a_rows[:, R], r_t1[:, :], AF.Exp,
                                           scale=-0.5),
               AF.Exp, inc=se)
        kb.vec(lambda e: e.tensor_scalar(r_t2[:, :], r_v[:, :], 1.0 + EPS2,
                                         EPS1 * EPS2, ALU.mult, ALU.add),
               inc=ve, fence=True)
        kb.act(lambda e: e.activation(r_t2[:, :], r_t2[:, :], AF.Ln),
               AF.Ln, waits=(ve,), inc=se)
        kb.act(lambda e, R=R: e.activation(s2_rows[:, R], r_t2[:, :], AF.Exp,
                                           scale=-0.5),
               AF.Exp, inc=se)
        kb.vec(lambda e, R=R: e.scalar_tensor_tensor(
            nma_rows[:, R], r_m[:, :], -1.0, a_rows[:, R], ALU.mult, ALU.mult),
            waits=(se,), inc=ve, fence=True)
        kb.vec(lambda e, R=R: e.scalar_tensor_tensor(
            nms2_rows[:, R], r_m[:, :], -1.0, s2_rows[:, R], ALU.mult,
            ALU.mult), inc=ve, fence=True)
        kb.vec(lambda e, R=R: e.tensor_scalar_add(sc1_rows[:, R],
                                                  a_rows[:, R], 1.0),
               inc=ve, fence=True)

        for t in range(NT):
            kb.act(lambda e, t=t, s=s:
                   e.activation(xnb_tm[t][:, :], x_tm[t][:, :], AF.Identity,
                                scale=s2_rows[:, s * NT + t:s * NT + t + 1],
                                bias=nms2_rows[:, s * NT + t:s * NT + t + 1]),
                   AF.Identity, waits=(ve, pe), inc=se)
        for j in range(NJ):
            bank = j % 2
            for t in range(NT):
                transpose_to(bank, t * 128, xnb_tm[t][:, j * 128:(j + 1) * 128],
                             waits=(se, dq_g) if t == 0 else ())
            kb.act(lambda e, j=j, b=bank:
                   e.activation(xn2_fm[j][:, 3:515], pb16[b][:, 0:512],
                                AF.Identity),
                   AF.Identity, waits=(pe, ve), inc=se)
            bank_consumed(bank, se)

        xc_toks = []
        for j in range(NJ):
            bank = 2 + j % 2
            mms = [(diag_b[kk][j][:, :], xn2_fm[j][:, kk:kk + 512])
                   for kk in range(4)]
            mms.append((convb_b[j][0:1, :], ones_b[0:1, 0:512]))
            mm_group(bank, mms, waits=(se, ve, dq_g))
            xc_toks.append(kb.act(lambda e, j=j, b=bank:
                   e.activation(xcf[j][:, :], psum[b][:, 0:512], AF.Silu),
                   AF.Silu, waits=(pe,), inc=se))
            bank_consumed(bank, se)
            kb.act(lambda e, j=j: e.copy(xn2_fm[j][:, 0:3],
                                         xn2_fm[j][:, 512:515]),
                   AF.Identity, inc=se)

        for idx, oc in enumerate(list(range(0, 16))):
            bank = 4 + idx % 2
            mm_group(bank, [(ipw_b[k][:, oc * 128:(oc + 1) * 128],
                             xcf[k][:, :]) for k in range(NJ)],
                     waits=tuple(xc_toks))
            if oc >= 8:
                kb.act(lambda e, oc=oc, b=bank:
                       e.activation(v_b[oc - 8][:, :], psum[b][:, 0:512],
                                    AF.Identity),
                       AF.Identity, waits=(pe, ve), inc=se)
                bank_consumed(bank, se)
            else:
                kb.act(lambda e, oc=oc, b=bank:
                       e.activation(exp_t, psum[b][:, 0:512], AF.Exp,
                                    bias=dtb_s[oc][:, :]),
                       AF.Exp, waits=(pe,), inc=se)
                bank_consumed(bank, se)
                kb.act(lambda e: e.activation(spb[:, :], exp_t, AF.Ln,
                                              bias=1.0),
                       AF.Ln, waits=(ve,), inc=se)
                kb.vec(lambda e, oc=oc, dtP=dtP: e.tensor_scalar(
                    dtP[oc][:, :], spb[:, :], 0.001, 2.0, ALU.max, ALU.min),
                    waits=(se,), inc=ve)

        for idx, oc in enumerate(list(range(16, 24))):
            bank = 4 + idx % 2
            mm_group(bank, [(ipw_b[k][:, oc * 128:(oc + 1) * 128],
                             xcf[k][:, :]) for k in range(NJ)],
                     waits=tuple(xc_toks))
            kb.act(lambda e, oc=oc, b=bank:
                   e.activation(g_sp[oc - 16][:, :], psum[b][:, 0:512],
                                AF.Sigmoid),
                   AF.Sigmoid, waits=(pe, dq_g), inc=se)
            bank_consumed(bank, se)
        for j in range(NJ):
            kb.dma("gpsimd", d_sp_g.ap()[j, s], g_sp[j][:, :], dq_g,
                   waits=(se,))

        for j in range(4):
            kb.vec(lambda e, j=j: e.tensor_mul(rope_t[0][:, :], v_b[j][:, :],
                                               cos_fm[j][:, :]),
                   waits=(se,), inc=ve)
            kb.vec(lambda e, j=j: e.tensor_mul(rope_t[1][:, :],
                                               v_b[j + 4][:, :],
                                               sin_fm[j][:, :]), inc=ve)
            kb.vec(lambda e: e.tensor_sub(rope_t[2][:, :], rope_t[0][:, :],
                                          rope_t[1][:, :]), inc=ve)
            kb.vec(lambda e, j=j, dtP=dtP, viP=viP: e.tensor_mul(
                viP[j][:, :], rope_t[2][:, :], dtP[j][:, :]),
                waits=(dq_g,), inc=ve)
            kb.vec(lambda e, j=j: e.tensor_mul(rope_t[0][:, :], v_b[j][:, :],
                                               sin_fm[j][:, :]), inc=ve)
            kb.vec(lambda e, j=j: e.tensor_mul(rope_t[1][:, :],
                                               v_b[j + 4][:, :],
                                               cos_fm[j][:, :]), inc=ve)
            kb.vec(lambda e: e.tensor_add(rope_t[2][:, :], rope_t[0][:, :],
                                          rope_t[1][:, :]), inc=ve)
            kb.vec(lambda e, j=j, dtP=dtP, viP=viP: e.tensor_mul(
                viP[j + 4][:, :], rope_t[2][:, :], dtP[j + 4][:, :]),
                waits=(dq_g,), inc=ve)
        for j in range(NJ):
            kb.dma("gpsimd", d_sp_v.ap()[j, s], viP[j][:, :], dq_g,
                   waits=(ve,))

    def p1_tail(s):
        dtP = dt_b if s % 2 == 0 else dt_B
        viP = v_in if s % 2 == 0 else vi_B
        pairs = [(ci, j) for ci in range(CPS) for j in range(NJ)]
        for p0 in range(0, len(pairs), 2):
            pp = pairs[p0:p0 + 2]
            for ib, (ci, j) in enumerate(pp):
                kb.vec(lambda e, j=j, ci=ci, ib=ib, dtP=dtP:
                       e.tensor_scalar_mul(
                    la_b[ib], dtP[j][:, ci * CH:ci * CH + CH], -1.0), inc=ve)
            for ib, (ci, j) in enumerate(pp):
                kb.vec(lambda e, ib=ib: e.tensor_tensor_scan(
                    Lbuf[ib], la_b[ib], la_b[ib], 0.0,
                    ALU.add, ALU.bypass), inc=ve)
            for ib, (ci, j) in enumerate(pp):
                kb.vec(lambda e, ib=ib: e.tensor_scalar_max(
                    Lc_p[ib][:, 1:257], Lbuf[ib], -20.0), inc=ve)
            for ib, (ci, j) in enumerate(pp):
                c = s * CPS + ci
                kb.vec(lambda e, j=j, c=c, ib=ib: e.tensor_copy(
                    bdec[j][:, c:c + 1], Lc_p[ib][:, 256:257]), inc=ve)
            for ib, (ci, j) in enumerate(pp):
                kb.vec(lambda e, ib=ib: e.tensor_sub(
                    Lbuf[ib], Lc_p[ib][:, 1:257], Lc_p[ib][:, 0:256]),
                    inc=ve, fence=(ib == 1))
            for ib, (ci, j) in enumerate(pp):
                kb.act(lambda e, j=j, ib=ib, ci=ci:
                       e.activation(a_sp[j][:, ci * CH:ci * CH + CH],
                                    Lbuf[ib], AF.Exp),
                       AF.Exp, waits=(ve, dq_g), inc=se)
            for ib, (ci, j) in enumerate(pp):
                kb.vec(lambda e, j=j, ib=ib, ci=ci, viP=viP:
                       e.tensor_tensor_scan(
                    y0_b[ib], a_sp[j][:, ci * CH:ci * CH + CH],
                    viP[j][:, ci * CH:ci * CH + CH], 0.0, ALU.mult, ALU.add),
                    waits=(se,), inc=ve)
            for ib, (ci, j) in enumerate(pp):
                c = s * CPS + ci
                kb.vec(lambda e, j=j, c=c, ib=ib: e.tensor_copy(
                    bout[j][:, c:c + 1], y0_b[ib][:, 255:256]), inc=ve,
                    fence=(ib == 1))
        for j in range(NJ):
            kb.dma("gpsimd", d_sp_a.ap()[j, s], a_sp[j][:, :], dq_g,
                   waits=(se, ve))
        if dbg and s == 0:
            dj = int(os.environ.get("KDJ", "0"))
            dump("xn2", xn2_fm[dj][:, :], (se, ve))
            dump("xc", xcf[dj][:, :], (se,))
            dump("dt", dtP[dj][:, :], (ve,))
            dump("v", v_b[dj][:, :], (se,))
            dump("g", g_sp[dj][:, :], (se,))
            dump("vin", viP[dj][:, :], (ve,))
            dump("asp", a_sp[dj][:, :], (se,))
            dump("cos", cos_fm[dj % 4][:, :], (se,))

    for s in range(nseg1):
        p1_front(s)
        if s >= 1:
            p1_tail(s - 1)
    if nseg1 >= 1:
        p1_tail(nseg1 - 1)

    for j in (range(NJ) if stage >= 4 else ()):
        kb.vec(lambda e, j=j: e.tensor_tensor_scan(
            Ucum[:, :], bdec[j][:, :], bdec[j][:, :], 0.0, ALU.add,
            ALU.bypass), waits=(ve,), inc=ve, fence=True)
        kb.vec(lambda e, j=j: e.tensor_scalar(
            ncd_t[:, :], Ucum[:, :], bdec[j][:, 0:1], -20.0, ALU.subtract,
            ALU.max), waits=(se,), inc=ve, fence=True)
        kb.act(lambda e: e.activation(em_t[:, :], ncd_t[:, :], AF.Exp,
                                      scale=-1.0),
               AF.Exp, waits=(ve,), inc=se)
        kb.vec(lambda e, j=j: e.tensor_mul(seed_t[:, :], bout[j][:, :],
                                           em_t[:, :]),
               waits=(se,), inc=ve, fence=True)
        kb.vec(lambda e: e.tensor_reduce(contrib[:, 2:3], seed_t[:, :],
                                         mybir.AxisListType.X, ALU.add),
               waits=(dq_g,), inc=ve, fence=True)
        kb.vec(lambda e, j=j: e.tensor_copy(contrib[:, 0:1],
                                            bdec[j][:, 0:1]), inc=ve,
               fence=True)
        kb.vec(lambda e: e.tensor_copy(contrib[:, 1:2], Ucum[:, 15:16]),
               inc=ve, fence=True)
        kb.dma("gpsimd", d_ccin.ap()[j], contrib[:, :], dq_g, waits=(ve,))
        if dbg and j == int(os.environ.get("KDJ", "0")):
            dump("contrib", contrib[:, :], (ve,))

    if stage >= 4:
        def emit_cc(e):
            return e.collective_compute(
                "AllGather", ALU.bypass,
                replica_groups=[list(range(NCORES))],
                ins=[d_ccin.ap().opt()], outs=[d_ccout.ap().opt()])
        kb.step("gpsimd", emit_cc, waits=(dq_g,), inc=cc)

    ccv = d_ccout.ap().rearrange("(r j) p c -> r j p c", j=NJ)
    for j in (range(NJ) if stage >= 5 else ()):
        kb.dma("sync", gat[:, :].rearrange("p (c r) -> p c r", r=8),
               ccv[:, j].rearrange("r p c -> p c r"), dq_s,
               waits=(cc, ve), slow=True)
        kb.vec(lambda e: e.tensor_mul(gsel[:, :], gat[:, :], msk_s[:, :]),
               waits=(dq_s,), inc=ve, fence=True)
        for cci in range(3):
            kb.vec(lambda e, cci=cci: e.tensor_reduce(
                partner[:, cci:cci + 1], gsel[:, cci * 8:cci * 8 + 8],
                mybir.AxisListType.X, ALU.add), inc=ve, fence=True)
        kb.vec(lambda e, j=j: e.scalar_tensor_tensor(
            stabu[:, :], bdec[j][:, 0:1], even_s[:, 0:1], partner[:, 0:1],
            ALU.mult, ALU.add), inc=ve, fence=True)
        kb.vec(lambda e, j=j: e.tensor_tensor_scan(
            Ucum[:, :], bdec[j][:, :], bdec[j][:, :], partner[:, 1:2],
            ALU.add, ALU.bypass), inc=ve, fence=True)
        kb.vec(lambda e: e.tensor_scalar(
            ncd_t[:, :], Ucum[:, :], stabu[:, 0:1], -20.0, ALU.subtract,
            ALU.max), waits=(se,), inc=ve, fence=True)
        kb.act(lambda e: e.activation(em_t[:, :], ncd_t[:, :], AF.Exp,
                                      scale=-1.0),
               AF.Exp, waits=(ve,), inc=se)
        kb.act(lambda e: e.activation(ep_t[:, :], ncd_t[:, :], AF.Exp,
                                      scale=1.0),
               AF.Exp, inc=se)
        kb.vec(lambda e, j=j: e.tensor_mul(seed_t[:, :], bout[j][:, :],
                                           em_t[:, :]),
               waits=(se,), inc=ve, fence=True)
        kb.vec(lambda e, j=j: e.tensor_tensor_scan(
            incl_t[:, :], seed_t[:, :], seed_t[:, :], partner[:, 2:3],
            ALU.add, ALU.bypass), inc=ve, fence=True)
        kb.vec(lambda e: e.tensor_sub(incl_t[:, :], incl_t[:, :],
                                      seed_t[:, :]), inc=ve, fence=True)
        kb.vec(lambda e, j=j: e.tensor_mul(carry[j][:, :], incl_t[:, :],
                                           ep_t[:, :]), inc=ve, fence=True)
        if dbg and j == int(os.environ.get("KDJ", "0")):
            dump("gat", gat[:, :], (dq_s,))
            dump("part", partner[:, :], (ve,))
            dump("ucum", Ucum[:, :], (ve,))
            dump("ncd", ncd_t[:, :], (ve,))

    if dbg:
        dump("arows", a_rows[:, :], (se,))
        dump("s2rows", s2_rows[:, :], (se,))
        dump("bdec", bdec[0][:, :], (ve,))
        dump("bout", bout[0][:, :], (ve,))
        dump("carry", carry[0][:, :], (ve,))

    for j in range(NJ):
        kb.dma("gpsimd", rT_b[j][:, :], d_rTs.ap()[j], dq_g, waits=(ve, se))
    for l in range(2):
        kb.dma("gpsimd", lv2_b[l][:, :], d_lv2s.ap()[l], dq_g,
               waits=(ve, se))

    nseg2 = 0 if stage < 6 else (1 if stage == 6 else NSEG)
    sub = int(os.environ.get("KSUB", "9"))
    for s in range(nseg2):
        base = s * SEG
        for t in range(NT):
            kb.dma("sync", x_tm[t][:, :],
                   d_xe.ap()[3 + base + t * 128: 3 + base + (t + 1) * 128, :],
                   dq_s, waits=(se, ve))
        for j in range(NJ):
            kb.dma("gpsimd", g_sp[j][:, :], d_sp_g.ap()[j, s], dq_g,
                   waits=(ve,))
            kb.dma("gpsimd", v_in[j][:, :], d_sp_v.ap()[j, s], dq_g,
                   waits=(ve,))
            kb.dma("gpsimd", a_sp[j][:, :], d_sp_a.ap()[j, s], dq_g,
                   waits=(ve,))

        for t in range(NT):
            col = slice(s * NT + t, s * NT + t + 1)
            kb.act(lambda e, t=t, col=col:
                   e.activation(S_tm[t][:, :], x_tm[t][:, :], AF.Identity,
                                scale=sc1_rows[:, col],
                                bias=nma_rows[:, col]),
                   AF.Identity, waits=(dq_s, pe, ve), inc=se)
            kb.act(lambda e, t=t, col=col:
                   e.activation(xnb_tm[t][:, :], x_tm[t][:, :], AF.Identity,
                                scale=a_rows[:, col], bias=nma_rows[:, col]),
                   AF.Identity, inc=se)
        for j in (range(NJ) if sub >= 2 else ()):
            bank = j % 2
            for t in range(NT):
                transpose_to(bank, t * 128, xnb_tm[t][:, j * 128:(j + 1) * 128],
                             waits=(se,))
            kb.act(lambda e, j=j, b=bank:
                   e.activation(xn_fm[j], pb16[b][:, 0:512], AF.Identity),
                   AF.Identity, waits=(pe,), inc=se)
            bank_consumed(bank, se)

        for zc in (range(2) if sub >= 3 else ()):
            bank = 2 + zc
            mm_group(bank, [(rT_b[k][:, zc * 128:(zc + 1) * 128],
                             xn_fm[k]) for k in range(NJ)],
                     waits=(se, dq_g))
            kb.act(lambda e, zc=zc, b=bank:
                   e.activation(exp_t, psum[b][:, 0:512], AF.Exp),
                   AF.Exp, waits=(pe,), inc=se)
            kb.act(lambda e, zc=zc:
                   e.activation(pm_b[zc][:, :], exp_t, AF.Ln, bias=1.0),
                   AF.Ln, inc=se)
            kb.act(lambda e, zc=zc, b=bank:
                   e.activation(exp_t, psum[b][:, 0:512], AF.Exp,
                                scale=-1.0),
                   AF.Exp, inc=se)
            kb.act(lambda e, zc=zc:
                   e.activation(pm_b[2 + zc][:, :], exp_t, AF.Ln,
                                bias=1.0),
                   AF.Ln, inc=se)
            bank_consumed(bank, se)
        for lc in (range(2) if sub >= 3 else ()):
            bank = 2 + lc
            mm_group(bank, [(sel_b[lc * 4 + nk][:, :], pm_b[nk][:, :])
                            for nk in range(4)], waits=(se,))
            kb.act(lambda e, lc=lc, b=bank:
                   e.activation(lp_b[lc][:, :], psum[b][:, 0:512], AF.Exp),
                   AF.Exp, waits=(pe,), inc=se)
            bank_consumed(bank, se)

        for ci in (range(CPS) if sub >= 4 else ()):
            c = s * CPS + ci
            r0 = ci * CH
            for j in range(NJ):
                kb.vec(lambda e, j=j, c=c, r0=r0: e.tensor_tensor_scan(
                    xcf[j][:, r0:r0 + CH], a_sp[j][:, r0:r0 + CH],
                    v_in[j][:, r0:r0 + CH], carry[j][:, c:c + 1],
                    ALU.mult, ALU.add), waits=(dq_g, pe), inc=ve)
        _hm_order = list(range(NJ))
        if os.environ.get("KHMREV", "0") == "1":
            _hm_order = _hm_order[::-1]
        if sub >= 5 and os.environ.get("KHMDUM", "0") == "1":
            mm_group(4, [(hmbd_b[:, :], xcf[0][:, :])], waits=(ve,))
            kb.act(lambda e: e.activation(spb[:, :], psum[4][:, 0:512],
                                          AF.Identity),
                   AF.Identity, waits=(pe, ve), inc=se)
            bank_consumed(4, se)
        for j in (_hm_order if sub >= 5 else ()):
            bank = 4 + j % 2
            mm_group(bank, [(hmbd_b[:, :], xcf[j][:, :])], waits=(ve,))
            kb.act(lambda e, j=j, b=bank: e.activation(
                spb[:, :], psum[b][:, 0:512], AF.Identity),
                AF.Identity, waits=(pe, ve), inc=se)
            bank_consumed(bank, se)
            kb.vec(lambda e, j=j: e.tensor_mul(
                dt_b[j][:, :], g_sp[j][:, :], spb[:, :]),
                waits=(se, dq_g, pe), inc=ve)
            if dbg and s == 0 and j == 0:
                dump("vmixd", spb[:, :], (se,))
                dump("g2", g_sp[0][:, :], (dq_g,))

        for oc in (range(NJ) if sub >= 6 else ()):
            bank = 6 + oc % 2
            mms = [(opw_b[k][:, oc * 128:(oc + 1) * 128], dt_b[k][:, :])
                   for k in range(NJ)]
            mms += [(lv2_b[lc][:, oc * 128:(oc + 1) * 128], lp_b[lc][:, :])
                    for lc in range(2)]
            mm_group(bank, mms, waits=(ve, se, dq_g))
            kb.act(lambda e, oc=oc, b=bank:
                   e.activation(v_b[oc][:, :], psum[b][:, 0:512], AF.Identity),
                   AF.Identity, waits=(pe,), inc=se)
            bank_consumed(bank, se)
        for t in (range(NT) if sub >= 7 else ()):
            for hf in range(2):
                bank = (t * 2 + hf) % 2
                for q in range(4):
                    oc = hf * 4 + q
                    transpose_to(bank, q * 128,
                                 v_b[oc][:, t * 128:(t + 1) * 128],
                                 waits=(se, ve))
                kb.act(lambda e, b=bank: e.activation(
                    rope_t[b][:, :], pb16[b][:, 0:512], AF.Identity),
                    AF.Identity, waits=(pe, ve), inc=se)
                bank_consumed(bank, se)
                kb.vec(lambda e, t=t, hf=hf, b=bank: e.tensor_add(
                    x_tm[t][:, hf * 512:(hf + 1) * 512],
                    S_tm[t][:, hf * 512:(hf + 1) * 512],
                    rope_t[b][:, :]), waits=(se,), inc=ve)
            kb.dma("sync", d_y.ap()[base + t * 128: base + (t + 1) * 128, :],
                   x_tm[t][:, :], dq_s, waits=(ve,))
        if dbg and s == 0:
            dj = int(os.environ.get("KDJ", "0"))
            dump("pm0", pm_b[0][:, :], (se,))
            dump("pm2", pm_b[2][:, :], (se,))
            dump("lp", lp_b[0][:, :], (se,))
            dump("fin", xcf[dj][:, :], (ve,))
            dump("hmg", dt_b[dj][:, :], (ve,))
            dump("attn", v_b[dj][:, :], (se,))
            dump("stm", S_tm[0][:, :], (se,))

    kb.step("sync", lambda e: e.nop(), waits=(dq_s,))
    kb.step("gpsimd", lambda e: e.nop(), waits=(dq_g, dq_s))

    import contextlib
    with contextlib.ExitStack() as ctx:
        kb.emit(ctx)
    return nc


_NC_CACHE = []


def kernel(**inputs):
    in_maps = _host_build(inputs)
    if not _NC_CACHE:
        _NC_CACHE.append(_build_program())
    nc = _NC_CACHE[0]
    res = run_bass_kernel_spmd(nc, in_maps, core_ids=list(range(NCORES)))
    y = np.empty((B, N, D), dtype=np.float32)
    for c in range(NCORES):
        seq, half = c // 2, c % 2
        y[seq, half * TPC:(half + 1) * TPC] = res.results[c]["y"]
    return y


# version 10
# speedup vs baseline: 1.2159x; 1.2159x over previous
import os
import numpy as np
import concourse.bass as bass
import concourse.mybir as mybir
from concourse.bass_utils import run_bass_kernel_spmd

F32 = mybir.dt.float32
BF16 = mybir.dt.bfloat16
AF = mybir.ActivationFunctionType
ALU = mybir.AluOpType

NCORES = 8
B, N, D = 4, 8192, 1024
H, DH = 16, 64
TPC = 4096
SEG = 512
NSEG = TPC // SEG
CH = 256
CPS = SEG // CH
NCH = TPC // CH
NT = SEG // 128
NJ = D // 128
GAP = 4
EPS1 = 1e-5
EPS2 = 1e-6

_SET = {AF.Exp: "nle", AF.Ln: "nle", AF.Sigmoid: "sig", AF.Silu: "silu"}
ENGS = ("sync", "tensor", "vector", "scalar", "gpsimd")


class _Sem:
    __slots__ = ("name", "h", "count")

    def __init__(self, name):
        self.name, self.h, self.count = name, None, 0


class KB:

    def __init__(self, nc):
        self.nc = nc
        self.steps = {e: [] for e in ENGS}
        self.sems = []
        self.cur_set = None

    def sem(self, name):
        s = _Sem(name)
        self.sems.append(s)
        return s

    def step(self, eng, emit, waits=(), inc=None, inc_n=1, fence=False):
        w = []
        for x in waits:
            if isinstance(x, _Sem):
                if x.count > 0:
                    w.append((x, x.count))
            elif x is not None:
                s, v = x
                if v > 0:
                    w.append((s, v))
        if inc is not None:
            inc.count += inc_n
        self.steps[eng].append((emit, tuple(w), inc, inc_n, fence))
        return (inc, inc.count) if inc is not None else None

    def dma(self, eng, dst, src, sem, waits=(), slow=False):
        if slow:
            return self.step(eng, lambda e, d=dst, s=src: e.dma_start(
                out=d, in_=s, allow_slow_non_contiguous=True),
                waits=waits, inc=sem, inc_n=16)
        return self.step(eng, lambda e, d=dst, s=src: e.dma_start(
            out=d, in_=s), waits=waits, inc=sem, inc_n=16)

    def act(self, emitf, func, waits=(), inc=None):
        new_set = _SET.get(func)
        fence = new_set is not None and self.cur_set is not None \
            and new_set != self.cur_set
        if new_set is not None:
            self.cur_set = new_set

        def emit(e, f=emitf, nf=fence):
            if nf:
                e.drain()
            return f(e)
        return self.step("scalar", emit, waits=waits, inc=inc)

    def vec(self, emitf, waits=(), inc=None, fence=False):
        return self.step("vector", emitf, waits=waits, inc=inc, fence=fence)

    def emit(self, ctx):
        for s in self.sems:
            s.h = ctx.enter_context(self.nc.semaphore(s.name))
        blk = ctx.enter_context(self.nc.Block())
        for ename in ENGS:
            steps = self.steps[ename]

            def body(eng, _steps=steps, _e=ename):
                last = {}
                for emit, waits, inc, inc_n, fence in _steps:
                    for s, v in waits:
                        if last.get(s, 0) < v:
                            eng.wait_ge(s.h, v)
                            last[s] = v
                    ins = emit(eng)
                    if _e == "vector":
                        if inc is not None:
                            if ins is None:
                                ins = eng.nop()
                            ins.then_inc(inc.h, inc_n)
                        if fence:
                            eng.drain()
                        eng.nop()
                    elif inc is not None:
                        if ins is None:
                            ins = eng.nop()
                        ins.then_inc(inc.h, inc_n)
            getattr(blk, ename)(body)



def _host_build(inputs):
    x = np.ascontiguousarray(np.asarray(inputs["L0_attn_x"]), dtype=np.float32)
    ipw = np.asarray(inputs["L0_attn_in_proj_w"], dtype=np.float32)
    dtb = np.asarray(inputs["L0_attn_dt_bias"], dtype=np.float32)
    convw = np.asarray(inputs["L0_attn_conv_w"], dtype=np.float32)
    convb = np.asarray(inputs["L0_attn_conv_b"], dtype=np.float32)
    nsc = np.asarray(inputs["L0_attn_norm_scale"], dtype=np.float32)
    hm = np.asarray(inputs["L0_attn_head_mix"], dtype=np.float32)
    opw = np.asarray(inputs["L0_attn_out_proj_w"], dtype=np.float32)
    wq = np.asarray(inputs["L0_ffn_Wq"], dtype=np.float32)
    rt = np.asarray(inputs["L0_ffn_routers"], dtype=np.float32)
    lv = np.asarray(inputs["L0_ffn_leaf_values"], dtype=np.float32)
    wo = np.asarray(inputs["L0_ffn_Wout_w"], dtype=np.float32)
    wob = np.asarray(inputs["L0_ffn_Wout_b"], dtype=np.float32)

    p = np.arange(D)
    old = (p % 16) * 64 + (p // 16)

    ipw_p = np.ascontiguousarray(np.concatenate(
        [ipw[:, t * D:(t + 1) * D][:, old] for t in range(3)], axis=1))
    dtb_p = np.ascontiguousarray(dtb[old].reshape(D, 1))
    opw_p = np.ascontiguousarray(opw[old, :])

    convd = np.zeros((4, NJ, 128, 128), dtype=np.float32)
    for kk in range(4):
        for j in range(NJ):
            convd[kk, j][np.arange(128), np.arange(128)] = \
                convw[j * 128:(j + 1) * 128, 0, kk]
    convb_r = np.ascontiguousarray(convb.reshape(NJ, 1, 128))
    nsc_c = np.ascontiguousarray(nsc.reshape(D, 1))

    hmbd = np.zeros((128, 128), dtype=np.float32)
    for kk in range(8):
        hmbd[kk * 16:(kk + 1) * 16, kk * 16:(kk + 1) * 16] = hm

    rtT = np.zeros((D, 256), dtype=np.float32)
    rtT[:, :255] = rt.T
    lvT = np.ascontiguousarray(lv.T)
    woT = np.ascontiguousarray(wo.T)
    wob_r = np.ascontiguousarray(wob.reshape(1, D))

    sel = np.zeros((2, 4, 128, 128), dtype=np.float32)
    for leaf in range(256):
        lc, lcol = leaf // 128, leaf % 128
        for lev in range(8):
            node = (1 << lev) - 1 + (leaf >> (8 - lev))
            bit = (leaf >> (7 - lev)) & 1
            chunk = (node // 128) if bit == 0 else 2 + (node // 128)
            sel[lc, chunk, node % 128, lcol] = -1.0

    ropesel = np.zeros((4, 32, 128), dtype=np.float32)
    for j in range(4):
        for pp in range(128):
            ropesel[j, 8 * j + pp // 16, pp] = 1.0

    inv_freq = (1.0 / (10000.0 ** (np.arange(0, DH, 2, dtype=np.float32)
                                   / DH))).astype(np.float32)

    shared = dict(
        ipw=ipw_p, dtb=dtb_p, convd=convd, convb=convb_r, nsc=nsc_c,
        hmbd=hmbd, opw=opw_p, wq=np.ascontiguousarray(wq), rtT=rtT, lvT=lvT,
        woT=woT, wob=wob_r, sel=np.ascontiguousarray(sel.reshape(8, 128, 128)),
        ident=np.eye(128, dtype=np.float32),
        ropesel=np.ascontiguousarray(ropesel))

    in_maps = []
    for c in range(NCORES):
        seq, half = c // 2, c % 2
        t0 = half * TPC
        xe = np.zeros((3 + TPC, D), dtype=np.float32)
        if half == 1:
            xe[0:3] = x[seq, t0 - 3:t0]
        xe[3:] = x[seq, t0:t0 + TPC]

        tpos = np.arange(t0, t0 + TPC, dtype=np.float32)
        ang = tpos[None, :] * inv_freq[:, None]
        m = dict(shared)
        msk24 = np.zeros((128, 24), dtype=np.float32)
        if half == 1:
            for cc in range(3):
                msk24[:, cc * 8 + (c - 1)] = 1.0
        m.update(xe=np.ascontiguousarray(xe),
                 ropec=np.ascontiguousarray(np.cos(ang).astype(np.float32)),
                 ropes=np.ascontiguousarray(np.sin(ang).astype(np.float32)),
                 msk24=msk24,
                 evenflag=np.full((128, 1), 1.0 - half, dtype=np.float32))
        in_maps.append(m)
    return in_maps



def _build_program():
    nc = bass.Bass("TRN2", target_bir_lowering=False)

    def A(name, shape, dt=F32, kind="ExternalInput"):
        return nc.dram_tensor(name, list(shape), dt, kind=kind)

    d_xe = A("xe", (3 + TPC, D))
    d_ipw = A("ipw", (D, 3 * D))
    d_dtb = A("dtb", (D, 1))
    d_convd = A("convd", (4, NJ, 128, 128))
    d_convb = A("convb", (NJ, 1, 128))
    d_nsc = A("nsc", (D, 1))
    d_hmbd = A("hmbd", (128, 128))
    d_opw = A("opw", (D, D))
    d_wq = A("wq", (D, D))
    d_rtT = A("rtT", (D, 256))
    d_lvT = A("lvT", (D, 256))
    d_woT = A("woT", (D, D))
    d_wob = A("wob", (1, D))
    d_sel = A("sel", (8, 128, 128))
    d_ident = A("ident", (128, 128))
    d_ropec = A("ropec", (32, TPC))
    d_ropes = A("ropes", (32, TPC))
    d_ropesel = A("ropesel", (4, 32, 128))
    d_msk24 = A("msk24", (128, 24))
    d_even = A("evenflag", (128, 1))
    d_y = A("y", (TPC, D), F32, "ExternalOutput")
    dbg = os.environ.get("KDBG", "0") == "1"
    if dbg:
        _dbg_specs = [
            ("xn2", (128, 520), BF16), ("xc", (128, SEG), BF16),
            ("dt", (128, SEG), BF16), ("v", (128, SEG), BF16),
            ("g", (128, SEG), BF16), ("vin", (128, SEG), BF16),
            ("asp", (128, SEG), BF16), ("arows", (128, NT * NSEG), F32),
            ("s2rows", (128, NT * NSEG), F32), ("bdec", (128, NCH), F32),
            ("bout", (128, NCH), F32), ("carry", (128, NCH), F32),
            ("pm0", (128, SEG), BF16), ("pm2", (128, SEG), BF16),
            ("lp", (128, SEG), BF16), ("fin", (128, SEG), BF16),
            ("hmg", (128, SEG), BF16), ("attn", (128, SEG), BF16),
            ("stm", (128, D), F32), ("cos", (128, SEG), BF16),
            ("vmixd", (128, SEG), BF16), ("g2", (128, SEG), BF16),
            ("gat", (128, 24), F32), ("part", (128, 3), F32),
            ("contrib", (128, 3), F32), ("ucum", (128, NCH), F32),
            ("ncd", (128, NCH), F32)]
        d_dbg = {nm: A("dbg_" + nm, shp, dt, "ExternalOutput")
                 for nm, shp, dt in _dbg_specs}

        def dump(nm, src_ap, eng_waits):
            kb.dma("gpsimd", d_dbg[nm].ap(), src_ap, dq_g, waits=eng_waits)

    d_sp_g = nc.dram_tensor("sp_g", [NJ, NSEG, 128, SEG], BF16, kind="Internal")
    d_sp_v = nc.dram_tensor("sp_v", [NJ, NSEG, 128, SEG], BF16, kind="Internal")
    d_sp_a = nc.dram_tensor("sp_a", [NJ, NSEG, 128, SEG], BF16, kind="Internal")
    d_ccin = nc.dram_tensor("ccin", [NJ, 128, 3], F32, kind="Internal")
    d_ccout = nc.dram_tensor("ccout", [NCORES * NJ, 128, 3], F32,
                             kind="Internal")
    d_halo = nc.dram_tensor("halo", [3, D], BF16, kind="Internal")
    d_rTs = nc.dram_tensor("rTs", [NJ, 128, 256], BF16, kind="Internal")
    d_lv2s = nc.dram_tensor("lv2s", [2, 128, D], BF16, kind="Internal")

    def S(name, shape, dt):
        return nc.alloc_sbuf_tensor(name, list(shape), dt)

    x_tm = [S(f"x_tm{t}", (128, D), F32) for t in range(NT)]
    S_tm = [S(f"S_tm{t}", (128, D), F32) for t in range(NT)]
    xnb_tm = [S(f"xnb_tm{t}", (128, D), BF16) for t in range(NT)]
    xn2_fm = [S(f"xn2_fm{j}", (128, 520), BF16) for j in range(NJ)]
    xcf = [S(f"xcf{j}", (128, SEG), BF16) for j in range(NJ)]
    v_in = [S(f"v_in{j}", (128, SEG), BF16) for j in range(NJ)]
    a_sp = [S(f"a_sp{j}", (128, SEG), BF16) for j in range(NJ)]
    g_sp = [S(f"g_sp{j}", (128, SEG), BF16) for j in range(NJ)]
    dt_b = [S(f"dt_b{j}", (128, SEG), BF16) for j in range(NJ)]
    v_b = [S(f"v_b{j}", (128, SEG), BF16) for j in range(NJ)]
    cos_fm = [S(f"cos_fm{j}", (128, SEG), BF16) for j in range(4)]
    sin_fm = [S(f"sin_fm{j}", (128, SEG), BF16) for j in range(4)]
    rope_t = [S(f"rope_t{i}", (128, SEG), BF16) for i in range(3)]
    stg = [S(f"stg{i}", (128, 768), F32) for i in range(2)]
    small = S("small_stg", (128, 128), F32)
    spb = rope_t[2]

    xn_fm = [xn2_fm[j][:, 4:516] for j in range(NJ)]
    pm_b = cos_fm
    lp_b = sin_fm
    exp_t = stg[0][:, 0:512]
    cstg = [stg[1][0:32, 0:512], None]
    halo_x = S_tm[0][0:3, :]
    halo_dum = S_tm[1][0:3, :]
    halo_tm = xnb_tm[0][0:3, :]

    def S_at(base_th, byte_off, name, shape, dt):
        addr = nc.lookup_mloc(base_th).addr + byte_off
        return nc.alloc_sbuf_tensor_at(name, list(shape), dt, offset=addr)

    csl_b = [S_at(stg[1], 2048, "csl_b0", (32, SEG), BF16),
             None]
    cstg[1] = S_at(xnb_tm[3], 0, "cstg1_f32", (32, 512), F32)[:, :]
    dt_B = [S_at(S_tm[2 + k // 4], (k % 4) * 1024, f"dt_B{k}", (128, SEG),
                 BF16) for k in range(NJ)]
    vi_B = [S_at(S_tm[k // 4], (k % 4) * 1024, f"vi_B{k}", (128, SEG), BF16)
            for k in range(NJ)]

    ipw_b = [S(f"ipw_b{k}", (128, 3 * D), BF16) for k in range(NJ)]
    opw_b = [S(f"opw_b{k}", (128, D), BF16) for k in range(NJ)]
    rT_b = [S(f"rT_b{j}", (128, 256), BF16) for j in range(NJ)]
    lv2_b = [S(f"lv2_b{l}", (128, D), BF16) for l in range(2)]
    sel_b = [S(f"sel_b{i}", (128, 128), BF16) for i in range(8)]
    diag_b = [[S(f"diag_b{k}_{j}", (128, 128), BF16) for j in range(NJ)]
              for k in range(4)]
    hmbd_b = S("hmbd_b", (128, 128), BF16)
    ident_b = S("ident_b", (128, 128), BF16)
    ropesel_b = [S(f"ropesel_b{j}", (32, 128), BF16) for j in range(4)]
    convb_b = [S(f"convb_b{j}", (1, 128), BF16) for j in range(NJ)]
    wob_b = S("wob_b", (1, D), BF16)
    dtb_s = [S(f"dtb_s{j}", (128, 1), F32) for j in range(NJ)]
    nsc_s = [S(f"nsc_s{j}", (128, 1), F32) for j in range(NJ)]
    ones_b = S("ones_b", (1, SEG), BF16)
    msk_s = S("msk_s", (128, 24), F32)
    even_s = S("even_s", (128, 1), F32)

    csl_b[1] = S("csl_b1", (32, SEG), BF16)

    wq_b = [S_at(S_tm[k // 2], (k % 2) * 2048, f"wq_b{k}", (128, D), BF16)
            for k in range(NJ)]
    woT_b = [S_at(x_tm[k // 2], (k % 2) * 2048, f"woT_b{k}", (128, D), BF16)
             for k in range(NJ)]
    rtT_b = [S_at(xnb_tm[k // 4], (k % 4) * 512, f"rtT_b{k}", (128, 256),
                  BF16) for k in range(NJ)]
    lvT_b = [S_at(xnb_tm[2 + k // 4], (k % 4) * 512, f"lvT_b{k}", (128, 256),
                  BF16) for k in range(NJ)]

    rT_addrs = [nc.lookup_mloc(t).addr for t in rT_b]
    assert all(rT_addrs[k + 1] - rT_addrs[k] == 512 for k in range(NJ - 1)), \
        "rT_b tiles not contiguous"
    Lbuf_t = S_at(lv2_b[0], 0, "lbuf_f32", (128, 512), F32)
    Lbuf = [Lbuf_t[:, 0:CH], Lbuf_t[:, CH:2 * CH]]
    Lc_p = [S_at(lv2_b[1], 0, "lcp0_f32", (128, 257), F32)[:, :],
            S_at(rT_b[0], 0, "lcp1_f32", (128, 257), F32)[:, :]]
    la_t = S_at(rT_b[0], 1056, "la_b16", (128, 512), BF16)
    la_b = [la_t[:, 0:CH], la_t[:, CH:2 * CH]]
    y0_t = S_at(rT_b[0], 2080, "y0_b16", (128, 512), BF16)
    y0_b = [y0_t[:, 0:CH], y0_t[:, CH:2 * CH]]

    a_rows = S("a_rows", (128, NT * NSEG), F32)
    nma_rows = S("nma_rows", (128, NT * NSEG), F32)
    s2_rows = S("s2_rows", (128, NT * NSEG), F32)
    nms2_rows = S("nms2_rows", (128, NT * NSEG), F32)
    sc1_rows = S("sc1_rows", (128, NT * NSEG), F32)
    sum_c = S("sum_c", (128, NT), F32)
    sq_c = S("sq_c", (128, NT), F32)
    r_m = S("r_m", (128, NT), F32)
    r_v = S("r_v", (128, NT), F32)
    r_t1 = S("r_t1", (128, NT), F32)
    r_t2 = S("r_t2", (128, NT), F32)
    bdec = [S(f"bdec{j}", (128, NCH), F32) for j in range(NJ)]
    bout = [S(f"bout{j}", (128, NCH), F32) for j in range(NJ)]
    carry = [S(f"carry{j}", (128, NCH), F32) for j in range(NJ)]
    Ucum = S("Ucum", (128, NCH), F32)
    ncd_t = S("ncd_t", (128, NCH), F32)
    em_t = S("em_t", (128, NCH), F32)
    ep_t = S("ep_t", (128, NCH), F32)
    seed_t = S("seed_t", (128, NCH), F32)
    incl_t = S("incl_t", (128, NCH), F32)
    contrib = S("contrib", (128, 3), F32)
    gat = S("gat", (128, 24), F32)
    gsel = S("gsel", (128, 24), F32)
    partner = S("partner", (128, 3), F32)
    stabu = S("stabu", (128, 1), F32)
    hsum = S("hsum", (3, 2), F32)
    hrow = S("hrow", (3, 4), F32)

    pb16 = [nc.alloc_psum_tensor(f"pt16_{i}", [128, 1024], BF16)
            for i in range(2)]
    psum = [None, None] + [nc.alloc_psum_tensor(f"pb{i}", [128, 512], F32)
                           for i in range(2, 8)]

    stage = int(os.environ.get("KSTAGE", "9"))
    kb = KB(nc)
    dq_s = kb.sem("dq_s")
    dq_g = kb.sem("dq_g")
    pe = kb.sem("pe")
    ve = kb.sem("ve")
    se = kb.sem("se")
    cc = kb.sem("cc")

    bank_free = [None] * 8

    def mm_group(bank, mms, waits=()):
        bf = bank_free[bank]
        bw = waits

        def emit(e, _mms=tuple(mms), _bf=bf, _b=bank):
            if _bf is not None:
                e.wait_ge(_bf[0].h, _bf[1])
            n = len(_mms)
            ins = None
            for i, (lt, rh) in enumerate(_mms):
                if i > 0:
                    for _ in range(GAP):
                        e.nop()
                w = rh.shape[-1]
                ins = e.matmul(psum[_b][:, 0:w], lt, rh,
                               start=(i == 0), stop=(i == n - 1))
            return ins
        kb.step("tensor", emit, waits=waits, inc=pe)

    def bank_consumed(bank, sem):
        bank_free[bank] = (sem, sem.count)

    def transpose_to(bank, col, src_ap, waits=()):
        bf = bank_free[bank]

        def emit(e, _bf=bf, _c=col, _s=src_ap, _b=bank):
            if _bf is not None:
                e.wait_ge(_bf[0].h, _bf[1])
            return e.transpose(pb16[_b][:, _c:_c + 128], _s, ident_b[:, :])
        kb.step("tensor", emit, waits=waits, inc=pe)

    cnt = [0]

    def load_cast(dst_tiles, dram_ap, cols):
        piece = 768
        for k, dst in enumerate(dst_tiles):
            for p0 in range(0, cols, piece):
                pw = min(piece, cols - p0)
                sbuf = stg[cnt[0] % 2]
                cnt[0] += 1
                kb.dma("gpsimd", sbuf[:, 0:pw],
                       dram_ap[k * 128:(k + 1) * 128, p0:p0 + pw], dq_g,
                       waits=(ve,))
                kb.vec(lambda e, d=dst, s=sbuf, a=p0, w=pw:
                       e.tensor_copy(d[:, a:a + w], s[:, 0:w]),
                       waits=(dq_g,), inc=ve)

    load_cast(ipw_b, d_ipw.ap(), 3 * D)
    load_cast(opw_b, d_opw.ap(), D)
    load_cast(wq_b, d_wq.ap(), D)
    load_cast(rtT_b, d_rtT.ap(), 256)
    load_cast(lvT_b, d_lvT.ap(), 256)
    load_cast(woT_b, d_woT.ap(), D)

    def small_cast(dst, src_ap, pdim=128, scale=None):
        wtot = src_ap.shape[-1]
        for p0 in range(0, wtot, 128):
            w = min(128, wtot - p0)
            kb.dma("gpsimd", small[0:pdim, 0:w], src_ap[:, p0:p0 + w], dq_g,
                   waits=(ve, se))
            if scale is None:
                kb.vec(lambda e, d=dst, p=pdim, w=w, a=p0:
                       e.tensor_copy(d[0:p, a:a + w], small[0:p, 0:w]),
                       waits=(dq_g,), inc=ve)
            else:
                kb.act(lambda e, d=dst, p=pdim, w=w, a=p0, sc=scale:
                       e.activation(d[0:p, a:a + w], small[0:p, 0:w],
                                    AF.Identity, scale=sc[:, :]),
                       AF.Identity, waits=(dq_g,), inc=se)

    for j in range(NJ):
        kb.dma("gpsimd", nsc_s[j][:, :],
               d_nsc.ap()[j * 128:(j + 1) * 128, :], dq_g)
        kb.dma("gpsimd", dtb_s[j][:, :],
               d_dtb.ap()[j * 128:(j + 1) * 128, :], dq_g)
    kb.dma("gpsimd", msk_s[:, :], d_msk24.ap(), dq_g)
    kb.dma("gpsimd", even_s[:, :], d_even.ap(), dq_g)

    small_cast(hmbd_b, d_hmbd.ap())
    small_cast(ident_b, d_ident.ap())
    small_cast(wob_b, d_wob.ap(), pdim=1)
    for i in range(8):
        small_cast(sel_b[i], d_sel.ap()[i])
    for j in range(4):
        small_cast(ropesel_b[j], d_ropesel.ap()[j], pdim=32)
    for j in range(NJ):
        small_cast(convb_b[j], d_convb.ap()[j], pdim=1)
    for kk in range(4):
        for j in range(NJ):
            small_cast(diag_b[kk][j], d_convd.ap()[kk, j], scale=nsc_s[j])

    kb.vec(lambda e: e.memset(ones_b[:, :], 1.0), inc=ve)

    for j in range(NJ):
        bank = 2 + j % 2
        mm_group(bank, [(wq_b[k][:, j * 128:(j + 1) * 128],
                         rtT_b[k][:, 0:256]) for k in range(NJ)],
                 waits=(ve, se))
        kb.act(lambda e, j=j, b=bank:
               e.activation(rT_b[j][:, :], psum[b][:, 0:256], AF.Identity),
               AF.Identity, waits=(pe,), inc=se)
        bank_consumed(bank, se)
    for lc in range(2):
        for hf in range(2):
            bank = 4 + (lc * 2 + hf) % 2
            mms = [(lvT_b[k][:, lc * 128:(lc + 1) * 128],
                    woT_b[k][:, hf * 512:(hf + 1) * 512]) for k in range(NJ)]
            mms.append((ones_b[0:1, 0:128],
                        wob_b[0:1, hf * 512:(hf + 1) * 512]))
            mm_group(bank, mms, waits=(ve, se))
            kb.act(lambda e, lc=lc, hf=hf, b=bank:
                   e.activation(lv2_b[lc][:, hf * 512:(hf + 1) * 512],
                                psum[b][:, 0:512], AF.Identity),
                   AF.Identity, waits=(pe,), inc=se)
            bank_consumed(bank, se)

    for j in range(NJ):
        kb.dma("gpsimd", d_rTs.ap()[j], rT_b[j][:, :], dq_g, waits=(se,))
    for l in range(2):
        kb.dma("gpsimd", d_lv2s.ap()[l], lv2_b[l][:, :], dq_g, waits=(se,))
    for i in range(2):
        kb.vec(lambda e, i=i: e.memset(Lc_p[i][:, 0:1], 0.0),
               waits=(dq_g,), inc=ve)

    _halo_holder = [None]
    if stage < 1:
        halo_on = False
    else:
        halo_on = True
    if halo_on:
     kb.dma("sync", halo_x, d_xe.ap()[0:3, :], dq_s, waits=(pe, se))
     kb.act(lambda e: e.activation(halo_dum, halo_x, AF.Identity,
                                   accum_out=hsum[:, 0:1]),
            AF.Identity, waits=(dq_s, pe), inc=se)
     kb.act(lambda e: e.activation(halo_dum, halo_x, AF.Square,
                                   accum_out=hsum[:, 1:2]),
            AF.Square, inc=se)
     kb.vec(lambda e: e.tensor_scalar_mul(hrow[:, 0:1], hsum[:, 0:1], 1.0 / D),
            waits=(se,), inc=ve, fence=True)
     kb.vec(lambda e: e.tensor_scalar_mul(hrow[:, 1:2], hsum[:, 1:2], 1.0 / D),
            inc=ve, fence=True)
     kb.vec(lambda e: e.tensor_mul(hrow[:, 2:3], hrow[:, 0:1], hrow[:, 0:1]),
            inc=ve, fence=True)
     kb.vec(lambda e: e.tensor_sub(hrow[:, 1:2], hrow[:, 1:2], hrow[:, 2:3]),
            inc=ve, fence=True)
     kb.vec(lambda e: e.tensor_scalar(hrow[:, 1:2], hrow[:, 1:2], 1.0 + EPS2,
                                      EPS1 * EPS2, ALU.mult, ALU.add),
            inc=ve, fence=True)
     kb.act(lambda e: e.activation(hrow[:, 2:3], hrow[:, 1:2], AF.Ln),
            AF.Ln, waits=(ve,), inc=se)
     kb.act(lambda e: e.activation(hrow[:, 2:3], hrow[:, 2:3], AF.Exp,
                                   scale=-0.5),
            AF.Exp, inc=se)
     kb.vec(lambda e: e.scalar_tensor_tensor(hrow[:, 3:4], hrow[:, 0:1], -1.0,
                                             hrow[:, 2:3], ALU.mult, ALU.mult),
            waits=(se,), inc=ve, fence=True)
     kb.act(lambda e: e.activation(halo_tm, halo_x, AF.Identity,
                                   scale=hrow[:, 2:3], bias=hrow[:, 3:4]),
            AF.Identity, waits=(ve,), inc=se)
     kb.dma("gpsimd", d_halo.ap(), halo_tm, dq_g, waits=(se,))
     for j in range(NJ):
         _ht = kb.dma("gpsimd", xn2_fm[j][:, 0:3],
                d_halo.ap()[:, j * 128:(j + 1) * 128].rearrange("a b -> b a"),
                dq_g, waits=(dq_g,), slow=True)
         _halo_holder[0] = _ht

    OC_ORDER = list(range(16, 24)) + list(range(8, 16)) + list(range(0, 8))

    nseg1 = 0 if stage < 2 else (1 if stage == 2 else NSEG)
    sub = int(os.environ.get("KSUB", "9"))
    toks = {"halo": _halo_holder[0]}

    def p1_front(s):
        dtP = dt_b if s % 2 == 0 else dt_B
        viP = v_in if s % 2 == 0 else vi_B
        base = s * SEG
        for t in range(NT):
            kb.dma("sync", x_tm[t][:, :],
                   d_xe.ap()[3 + base + t * 128: 3 + base + (t + 1) * 128, :],
                   dq_s, waits=(se, ve))
        csl_toks = []
        for i, dsrc in enumerate((d_ropec, d_ropes)):
            kb.dma("sync", cstg[i], dsrc.ap()[:, base:base + SEG],
                   dq_s, waits=(se, pe))
            csl_toks.append(kb.act(lambda e, i=i: e.activation(
                csl_b[i][:, :], cstg[i], AF.Identity),
                AF.Identity, waits=(dq_s,), inc=se))
        for j in range(4):
            for i, dst in enumerate((cos_fm, sin_fm)):
                bank = 6 + ((2 * j + i) % 2)
                mm_group(bank, [(ropesel_b[j][:, :], csl_b[i][:, :])],
                         waits=(csl_toks[i],))
                kb.act(lambda e, j=j, b=bank, d=dst:
                       e.activation(d[j][:, :], psum[b][:, 0:SEG], AF.Identity),
                       AF.Identity,
                       waits=(pe, toks.get("rope_done")), inc=se)
                bank_consumed(bank, se)

        for t in range(NT):
            kb.act(lambda e, t=t:
                   e.activation(exp_t, x_tm[t][:, 0:512], AF.Identity,
                                accum_out=sum_c[:, t:t + 1]),
                   AF.Identity,
                   waits=(dq_s, toks.get("rows_done")), inc=se)
            kb.act(lambda e, t=t:
                   e.activation(exp_t, x_tm[t][:, 512:1024], AF.Identity,
                                accum_out=r_t1[:, t:t + 1]),
                   AF.Identity, inc=se)
            kb.act(lambda e, t=t:
                   e.activation(exp_t, x_tm[t][:, 0:512], AF.Square,
                                accum_out=sq_c[:, t:t + 1]),
                   AF.Square, inc=se)
            kb.act(lambda e, t=t:
                   e.activation(exp_t, x_tm[t][:, 512:1024], AF.Square,
                                accum_out=r_t2[:, t:t + 1]),
                   AF.Square, inc=se)
        R = slice(s * NT, (s + 1) * NT)
        kb.vec(lambda e: e.tensor_add(sum_c[:, :], sum_c[:, :], r_t1[:, :]),
               waits=(se,), inc=ve, fence=True)
        kb.vec(lambda e: e.tensor_add(sq_c[:, :], sq_c[:, :], r_t2[:, :]),
               inc=ve, fence=True)
        kb.vec(lambda e: e.tensor_scalar_mul(r_m[:, :], sum_c[:, :], 1.0 / D),
               inc=ve, fence=True)
        kb.vec(lambda e: e.tensor_scalar_mul(r_v[:, :], sq_c[:, :], 1.0 / D),
               inc=ve, fence=True)
        kb.vec(lambda e: e.tensor_mul(r_t1[:, :], r_m[:, :], r_m[:, :]),
               inc=ve, fence=True)
        kb.vec(lambda e: e.tensor_sub(r_v[:, :], r_v[:, :], r_t1[:, :]),
               inc=ve, fence=True)
        kb.vec(lambda e: e.tensor_scalar_add(r_t1[:, :], r_v[:, :], EPS1),
               inc=ve, fence=True)
        kb.act(lambda e: e.activation(r_t1[:, :], r_t1[:, :], AF.Ln),
               AF.Ln, waits=(ve,), inc=se)
        kb.act(lambda e, R=R: e.activation(a_rows[:, R], r_t1[:, :], AF.Exp,
                                           scale=-0.5),
               AF.Exp, inc=se)
        kb.vec(lambda e: e.tensor_scalar(r_t2[:, :], r_v[:, :], 1.0 + EPS2,
                                         EPS1 * EPS2, ALU.mult, ALU.add),
               inc=ve, fence=True)
        kb.act(lambda e: e.activation(r_t2[:, :], r_t2[:, :], AF.Ln),
               AF.Ln, waits=(ve,), inc=se)
        kb.act(lambda e, R=R: e.activation(s2_rows[:, R], r_t2[:, :], AF.Exp,
                                           scale=-0.5),
               AF.Exp, inc=se)
        kb.vec(lambda e, R=R: e.scalar_tensor_tensor(
            nma_rows[:, R], r_m[:, :], -1.0, a_rows[:, R], ALU.mult, ALU.mult),
            waits=(se,), inc=ve, fence=True)
        kb.vec(lambda e, R=R: e.scalar_tensor_tensor(
            nms2_rows[:, R], r_m[:, :], -1.0, s2_rows[:, R], ALU.mult,
            ALU.mult), inc=ve, fence=True)
        toks["rows_done"] = kb.vec(lambda e, R=R: e.tensor_scalar_add(
            sc1_rows[:, R], a_rows[:, R], 1.0), inc=ve, fence=True)

        aff_toks = []
        for t in range(NT):
            aff_toks.append(kb.act(lambda e, t=t, s=s:
                   e.activation(xnb_tm[t][:, :], x_tm[t][:, :], AF.Identity,
                                scale=s2_rows[:, s * NT + t:s * NT + t + 1],
                                bias=nms2_rows[:, s * NT + t:s * NT + t + 1]),
                   AF.Identity,
                   waits=(toks.get("rows_done"), pe), inc=se))
        xn2e_toks = []
        for j in range(NJ):
            bank = j % 2
            for t in range(NT):
                transpose_to(bank, t * 128, xnb_tm[t][:, j * 128:(j + 1) * 128],
                             waits=(aff_toks[t],) if bank_free[bank] else
                             (aff_toks[t], dq_g))
            xn2e_toks.append(kb.act(lambda e, j=j, b=bank:
                   e.activation(xn2_fm[j][:, 3:515], pb16[b][:, 0:512],
                                AF.Identity),
                   AF.Identity, waits=(pe,), inc=se))
            bank_consumed(bank, se)

        xc_toks = []
        tc_new = []
        for j in range(NJ):
            bank = 2 + j % 2
            mms = [(diag_b[kk][j][:, :], xn2_fm[j][:, kk:kk + 512])
                   for kk in range(4)]
            mms.append((convb_b[j][0:1, :], ones_b[0:1, 0:512]))
            mm_group(bank, mms,
                     waits=(xn2e_toks[j], toks.get(("tail", j)),
                            toks.get("halo") if s == 0 else None))
            xc_toks.append(kb.act(lambda e, j=j, b=bank:
                   e.activation(xcf[j][:, :], psum[b][:, 0:512], AF.Silu),
                   AF.Silu, waits=(pe,), inc=se))
            bank_consumed(bank, se)
            tc_new.append(kb.act(lambda e, j=j: e.copy(xn2_fm[j][:, 0:3],
                                                       xn2_fm[j][:, 512:515]),
                   AF.Identity, inc=se))
        for j in range(NJ):
            toks[("tail", j)] = tc_new[j]

        for idx, oc in enumerate(list(range(0, 16))):
            bank = 4 + idx % 2
            mm_group(bank, [(ipw_b[k][:, oc * 128:(oc + 1) * 128],
                             xcf[k][:, :]) for k in range(NJ)],
                     waits=tuple(xc_toks))
            if oc >= 8:
                kb.act(lambda e, oc=oc, b=bank:
                       e.activation(v_b[oc - 8][:, :], psum[b][:, 0:512],
                                    AF.Identity),
                       AF.Identity,
                       waits=(pe, toks.get("rope_done")), inc=se)
                bank_consumed(bank, se)
            else:
                kb.act(lambda e, oc=oc, b=bank:
                       e.activation(exp_t, psum[b][:, 0:512], AF.Exp,
                                    bias=dtb_s[oc][:, :]),
                       AF.Exp, waits=(pe,), inc=se)
                bank_consumed(bank, se)
                kb.act(lambda e: e.activation(spb[:, :], exp_t, AF.Ln,
                                              bias=1.0),
                       AF.Ln, waits=(ve,), inc=se)
                kb.vec(lambda e, oc=oc, dtP=dtP: e.tensor_scalar(
                    dtP[oc][:, :], spb[:, :], 0.001, 2.0, ALU.max, ALU.min),
                    waits=(se,), inc=ve)

        for idx, oc in enumerate(list(range(16, 24))):
            bank = 4 + idx % 2
            mm_group(bank, [(ipw_b[k][:, oc * 128:(oc + 1) * 128],
                             xcf[k][:, :]) for k in range(NJ)],
                     waits=tuple(xc_toks))
            kb.act(lambda e, oc=oc, b=bank:
                   e.activation(g_sp[oc - 16][:, :], psum[b][:, 0:512],
                                AF.Sigmoid),
                   AF.Sigmoid, waits=(pe, dq_g), inc=se)
            bank_consumed(bank, se)
        for j in range(NJ):
            kb.dma("gpsimd", d_sp_g.ap()[j, s], g_sp[j][:, :], dq_g,
                   waits=(se,))

        for j in range(4):
            kb.vec(lambda e, j=j: e.tensor_mul(rope_t[0][:, :], v_b[j][:, :],
                                               cos_fm[j][:, :]),
                   waits=(se,), inc=ve)
            kb.vec(lambda e, j=j: e.tensor_mul(rope_t[1][:, :],
                                               v_b[j + 4][:, :],
                                               sin_fm[j][:, :]), inc=ve)
            kb.vec(lambda e: e.tensor_sub(rope_t[2][:, :], rope_t[0][:, :],
                                          rope_t[1][:, :]), inc=ve)
            kb.vec(lambda e, j=j, dtP=dtP, viP=viP: e.tensor_mul(
                viP[j][:, :], rope_t[2][:, :], dtP[j][:, :]),
                waits=(dq_g,), inc=ve)
            kb.vec(lambda e, j=j: e.tensor_mul(rope_t[0][:, :], v_b[j][:, :],
                                               sin_fm[j][:, :]), inc=ve)
            kb.vec(lambda e, j=j: e.tensor_mul(rope_t[1][:, :],
                                               v_b[j + 4][:, :],
                                               cos_fm[j][:, :]), inc=ve)
            kb.vec(lambda e: e.tensor_add(rope_t[2][:, :], rope_t[0][:, :],
                                          rope_t[1][:, :]), inc=ve)
            tok = kb.vec(lambda e, j=j, dtP=dtP, viP=viP: e.tensor_mul(
                viP[j + 4][:, :], rope_t[2][:, :], dtP[j + 4][:, :]),
                waits=(dq_g,), inc=ve)
        toks["rope_done"] = tok
        for j in range(NJ):
            kb.dma("gpsimd", d_sp_v.ap()[j, s], viP[j][:, :], dq_g,
                   waits=(ve,))

    def p1_tail(s):
        dtP = dt_b if s % 2 == 0 else dt_B
        viP = v_in if s % 2 == 0 else vi_B
        pairs = [(ci, j) for ci in range(CPS) for j in range(NJ)]
        for p0 in range(0, len(pairs), 2):
            pp = pairs[p0:p0 + 2]
            for ib, (ci, j) in enumerate(pp):
                kb.vec(lambda e, j=j, ci=ci, ib=ib, dtP=dtP:
                       e.tensor_scalar_mul(
                    la_b[ib], dtP[j][:, ci * CH:ci * CH + CH], -1.0), inc=ve)
            for ib, (ci, j) in enumerate(pp):
                kb.vec(lambda e, ib=ib: e.tensor_tensor_scan(
                    Lbuf[ib], la_b[ib], la_b[ib], 0.0,
                    ALU.add, ALU.bypass), inc=ve)
            for ib, (ci, j) in enumerate(pp):
                kb.vec(lambda e, ib=ib: e.tensor_scalar_max(
                    Lc_p[ib][:, 1:257], Lbuf[ib], -20.0), inc=ve)
            for ib, (ci, j) in enumerate(pp):
                c = s * CPS + ci
                kb.vec(lambda e, j=j, c=c, ib=ib: e.tensor_copy(
                    bdec[j][:, c:c + 1], Lc_p[ib][:, 256:257]), inc=ve)
            for ib, (ci, j) in enumerate(pp):
                kb.vec(lambda e, ib=ib: e.tensor_sub(
                    Lbuf[ib], Lc_p[ib][:, 1:257], Lc_p[ib][:, 0:256]),
                    inc=ve, fence=(ib == 1))
            for ib, (ci, j) in enumerate(pp):
                kb.act(lambda e, j=j, ib=ib, ci=ci:
                       e.activation(a_sp[j][:, ci * CH:ci * CH + CH],
                                    Lbuf[ib], AF.Exp),
                       AF.Exp, waits=(ve, dq_g), inc=se)
            for ib, (ci, j) in enumerate(pp):
                kb.vec(lambda e, j=j, ib=ib, ci=ci, viP=viP:
                       e.tensor_tensor_scan(
                    y0_b[ib], a_sp[j][:, ci * CH:ci * CH + CH],
                    viP[j][:, ci * CH:ci * CH + CH], 0.0, ALU.mult, ALU.add),
                    waits=(se,), inc=ve)
            for ib, (ci, j) in enumerate(pp):
                c = s * CPS + ci
                kb.vec(lambda e, j=j, c=c, ib=ib: e.tensor_copy(
                    bout[j][:, c:c + 1], y0_b[ib][:, 255:256]), inc=ve,
                    fence=(ib == 1))
        for j in range(NJ):
            kb.dma("gpsimd", d_sp_a.ap()[j, s], a_sp[j][:, :], dq_g,
                   waits=(se, ve))
        if dbg and s == 0:
            dj = int(os.environ.get("KDJ", "0"))
            dump("xn2", xn2_fm[dj][:, :], (se, ve))
            dump("xc", xcf[dj][:, :], (se,))
            dump("dt", dtP[dj][:, :], (ve,))
            dump("v", v_b[dj][:, :], (se,))
            dump("g", g_sp[dj][:, :], (se,))
            dump("vin", viP[dj][:, :], (ve,))
            dump("asp", a_sp[dj][:, :], (se,))
            dump("cos", cos_fm[dj % 4][:, :], (se,))

    for s in range(nseg1):
        p1_front(s)
        p1_tail(s)

    for j in (range(NJ) if stage >= 4 else ()):
        kb.vec(lambda e, j=j: e.tensor_tensor_scan(
            Ucum[:, :], bdec[j][:, :], bdec[j][:, :], 0.0, ALU.add,
            ALU.bypass), waits=(ve,), inc=ve, fence=True)
        kb.vec(lambda e, j=j: e.tensor_scalar(
            ncd_t[:, :], Ucum[:, :], bdec[j][:, 0:1], -20.0, ALU.subtract,
            ALU.max), waits=(se,), inc=ve, fence=True)
        kb.act(lambda e: e.activation(em_t[:, :], ncd_t[:, :], AF.Exp,
                                      scale=-1.0),
               AF.Exp, waits=(ve,), inc=se)
        kb.vec(lambda e, j=j: e.tensor_mul(seed_t[:, :], bout[j][:, :],
                                           em_t[:, :]),
               waits=(se,), inc=ve, fence=True)
        kb.vec(lambda e: e.tensor_reduce(contrib[:, 2:3], seed_t[:, :],
                                         mybir.AxisListType.X, ALU.add),
               waits=(dq_g,), inc=ve, fence=True)
        kb.vec(lambda e, j=j: e.tensor_copy(contrib[:, 0:1],
                                            bdec[j][:, 0:1]), inc=ve,
               fence=True)
        kb.vec(lambda e: e.tensor_copy(contrib[:, 1:2], Ucum[:, 15:16]),
               inc=ve, fence=True)
        kb.dma("gpsimd", d_ccin.ap()[j], contrib[:, :], dq_g, waits=(ve,))
        if dbg and j == int(os.environ.get("KDJ", "0")):
            dump("contrib", contrib[:, :], (ve,))

    if stage >= 4:
        def emit_cc(e):
            return e.collective_compute(
                "AllGather", ALU.bypass,
                replica_groups=[list(range(NCORES))],
                ins=[d_ccin.ap().opt()], outs=[d_ccout.ap().opt()])
        kb.step("gpsimd", emit_cc, waits=(dq_g,), inc=cc)

    ccv = d_ccout.ap().rearrange("(r j) p c -> r j p c", j=NJ)
    for j in (range(NJ) if stage >= 5 else ()):
        kb.dma("sync", gat[:, :].rearrange("p (c r) -> p c r", r=8),
               ccv[:, j].rearrange("r p c -> p c r"), dq_s,
               waits=(cc, ve), slow=True)
        kb.vec(lambda e: e.tensor_mul(gsel[:, :], gat[:, :], msk_s[:, :]),
               waits=(dq_s,), inc=ve, fence=True)
        for cci in range(3):
            kb.vec(lambda e, cci=cci: e.tensor_reduce(
                partner[:, cci:cci + 1], gsel[:, cci * 8:cci * 8 + 8],
                mybir.AxisListType.X, ALU.add), inc=ve, fence=True)
        kb.vec(lambda e, j=j: e.scalar_tensor_tensor(
            stabu[:, :], bdec[j][:, 0:1], even_s[:, 0:1], partner[:, 0:1],
            ALU.mult, ALU.add), inc=ve, fence=True)
        kb.vec(lambda e, j=j: e.tensor_tensor_scan(
            Ucum[:, :], bdec[j][:, :], bdec[j][:, :], partner[:, 1:2],
            ALU.add, ALU.bypass), inc=ve, fence=True)
        kb.vec(lambda e: e.tensor_scalar(
            ncd_t[:, :], Ucum[:, :], stabu[:, 0:1], -20.0, ALU.subtract,
            ALU.max), waits=(se,), inc=ve, fence=True)
        kb.act(lambda e: e.activation(em_t[:, :], ncd_t[:, :], AF.Exp,
                                      scale=-1.0),
               AF.Exp, waits=(ve,), inc=se)
        kb.act(lambda e: e.activation(ep_t[:, :], ncd_t[:, :], AF.Exp,
                                      scale=1.0),
               AF.Exp, inc=se)
        kb.vec(lambda e, j=j: e.tensor_mul(seed_t[:, :], bout[j][:, :],
                                           em_t[:, :]),
               waits=(se,), inc=ve, fence=True)
        kb.vec(lambda e, j=j: e.tensor_tensor_scan(
            incl_t[:, :], seed_t[:, :], seed_t[:, :], partner[:, 2:3],
            ALU.add, ALU.bypass), inc=ve, fence=True)
        kb.vec(lambda e: e.tensor_sub(incl_t[:, :], incl_t[:, :],
                                      seed_t[:, :]), inc=ve, fence=True)
        kb.vec(lambda e, j=j: e.tensor_mul(carry[j][:, :], incl_t[:, :],
                                           ep_t[:, :]), inc=ve, fence=True)
        if dbg and j == int(os.environ.get("KDJ", "0")):
            dump("gat", gat[:, :], (dq_s,))
            dump("part", partner[:, :], (ve,))
            dump("ucum", Ucum[:, :], (ve,))
            dump("ncd", ncd_t[:, :], (ve,))

    if dbg:
        dump("arows", a_rows[:, :], (se,))
        dump("s2rows", s2_rows[:, :], (se,))
        dump("bdec", bdec[0][:, :], (ve,))
        dump("bout", bout[0][:, :], (ve,))
        dump("carry", carry[0][:, :], (ve,))

    for j in range(NJ):
        kb.dma("gpsimd", rT_b[j][:, :], d_rTs.ap()[j], dq_g, waits=(ve, se))
    for l in range(2):
        kb.dma("gpsimd", lv2_b[l][:, :], d_lv2s.ap()[l], dq_g,
               waits=(ve, se))

    nseg2 = 0 if stage < 6 else (1 if stage == 6 else NSEG)
    sub = int(os.environ.get("KSUB", "9"))
    for s in range(nseg2):
        base = s * SEG
        for t in range(NT):
            kb.dma("sync", x_tm[t][:, :],
                   d_xe.ap()[3 + base + t * 128: 3 + base + (t + 1) * 128, :],
                   dq_s, waits=(se, ve))
        for j in range(NJ):
            kb.dma("gpsimd", g_sp[j][:, :], d_sp_g.ap()[j, s], dq_g,
                   waits=(ve,))
            kb.dma("gpsimd", v_in[j][:, :], d_sp_v.ap()[j, s], dq_g,
                   waits=(ve,))
            kb.dma("gpsimd", a_sp[j][:, :], d_sp_a.ap()[j, s], dq_g,
                   waits=(ve,))

        for t in range(NT):
            col = slice(s * NT + t, s * NT + t + 1)
            kb.act(lambda e, t=t, col=col:
                   e.activation(S_tm[t][:, :], x_tm[t][:, :], AF.Identity,
                                scale=sc1_rows[:, col],
                                bias=nma_rows[:, col]),
                   AF.Identity, waits=(dq_s, pe, ve), inc=se)
            kb.act(lambda e, t=t, col=col:
                   e.activation(xnb_tm[t][:, :], x_tm[t][:, :], AF.Identity,
                                scale=a_rows[:, col], bias=nma_rows[:, col]),
                   AF.Identity, inc=se)
        for j in (range(NJ) if sub >= 2 else ()):
            bank = j % 2
            for t in range(NT):
                transpose_to(bank, t * 128, xnb_tm[t][:, j * 128:(j + 1) * 128],
                             waits=(se,))
            kb.act(lambda e, j=j, b=bank:
                   e.activation(xn_fm[j], pb16[b][:, 0:512], AF.Identity),
                   AF.Identity, waits=(pe,), inc=se)
            bank_consumed(bank, se)

        for zc in (range(2) if sub >= 3 else ()):
            bank = 2 + zc
            mm_group(bank, [(rT_b[k][:, zc * 128:(zc + 1) * 128],
                             xn_fm[k]) for k in range(NJ)],
                     waits=(se, dq_g))
            kb.act(lambda e, zc=zc, b=bank:
                   e.activation(exp_t, psum[b][:, 0:512], AF.Exp),
                   AF.Exp, waits=(pe,), inc=se)
            kb.act(lambda e, zc=zc:
                   e.activation(pm_b[zc][:, :], exp_t, AF.Ln, bias=1.0),
                   AF.Ln, inc=se)
            kb.act(lambda e, zc=zc, b=bank:
                   e.activation(exp_t, psum[b][:, 0:512], AF.Exp,
                                scale=-1.0),
                   AF.Exp, inc=se)
            kb.act(lambda e, zc=zc:
                   e.activation(pm_b[2 + zc][:, :], exp_t, AF.Ln,
                                bias=1.0),
                   AF.Ln, inc=se)
            bank_consumed(bank, se)
        for lc in (range(2) if sub >= 3 else ()):
            bank = 2 + lc
            mm_group(bank, [(sel_b[lc * 4 + nk][:, :], pm_b[nk][:, :])
                            for nk in range(4)], waits=(se,))
            kb.act(lambda e, lc=lc, b=bank:
                   e.activation(lp_b[lc][:, :], psum[b][:, 0:512], AF.Exp),
                   AF.Exp, waits=(pe,), inc=se)
            bank_consumed(bank, se)

        for ci in (range(CPS) if sub >= 4 else ()):
            c = s * CPS + ci
            r0 = ci * CH
            for j in range(NJ):
                kb.vec(lambda e, j=j, c=c, r0=r0: e.tensor_tensor_scan(
                    xcf[j][:, r0:r0 + CH], a_sp[j][:, r0:r0 + CH],
                    v_in[j][:, r0:r0 + CH], carry[j][:, c:c + 1],
                    ALU.mult, ALU.add), waits=(dq_g, pe), inc=ve)
        _hm_order = list(range(NJ))
        if os.environ.get("KHMREV", "0") == "1":
            _hm_order = _hm_order[::-1]
        if sub >= 5 and os.environ.get("KHMDUM", "0") == "1":
            mm_group(4, [(hmbd_b[:, :], xcf[0][:, :])], waits=(ve,))
            kb.act(lambda e: e.activation(spb[:, :], psum[4][:, 0:512],
                                          AF.Identity),
                   AF.Identity, waits=(pe, ve), inc=se)
            bank_consumed(4, se)
        for j in (_hm_order if sub >= 5 else ()):
            bank = 4 + j % 2
            mm_group(bank, [(hmbd_b[:, :], xcf[j][:, :])], waits=(ve,))
            kb.act(lambda e, j=j, b=bank: e.activation(
                spb[:, :], psum[b][:, 0:512], AF.Identity),
                AF.Identity, waits=(pe, ve), inc=se)
            bank_consumed(bank, se)
            kb.vec(lambda e, j=j: e.tensor_mul(
                dt_b[j][:, :], g_sp[j][:, :], spb[:, :]),
                waits=(se, dq_g, pe), inc=ve)
            if dbg and s == 0 and j == 0:
                dump("vmixd", spb[:, :], (se,))
                dump("g2", g_sp[0][:, :], (dq_g,))

        for oc in (range(NJ) if sub >= 6 else ()):
            bank = 6 + oc % 2
            mms = [(opw_b[k][:, oc * 128:(oc + 1) * 128], dt_b[k][:, :])
                   for k in range(NJ)]
            mms += [(lv2_b[lc][:, oc * 128:(oc + 1) * 128], lp_b[lc][:, :])
                    for lc in range(2)]
            mm_group(bank, mms, waits=(ve, se, dq_g))
            kb.act(lambda e, oc=oc, b=bank:
                   e.activation(v_b[oc][:, :], psum[b][:, 0:512], AF.Identity),
                   AF.Identity, waits=(pe,), inc=se)
            bank_consumed(bank, se)
        for t in (range(NT) if sub >= 7 else ()):
            for hf in range(2):
                bank = (t * 2 + hf) % 2
                for q in range(4):
                    oc = hf * 4 + q
                    transpose_to(bank, q * 128,
                                 v_b[oc][:, t * 128:(t + 1) * 128],
                                 waits=(se, ve))
                kb.act(lambda e, b=bank: e.activation(
                    rope_t[b][:, :], pb16[b][:, 0:512], AF.Identity),
                    AF.Identity, waits=(pe, ve), inc=se)
                bank_consumed(bank, se)
                kb.vec(lambda e, t=t, hf=hf, b=bank: e.tensor_add(
                    x_tm[t][:, hf * 512:(hf + 1) * 512],
                    S_tm[t][:, hf * 512:(hf + 1) * 512],
                    rope_t[b][:, :]), waits=(se,), inc=ve)
            kb.dma("sync", d_y.ap()[base + t * 128: base + (t + 1) * 128, :],
                   x_tm[t][:, :], dq_s, waits=(ve,))
        if dbg and s == 0:
            dj = int(os.environ.get("KDJ", "0"))
            dump("pm0", pm_b[0][:, :], (se,))
            dump("pm2", pm_b[2][:, :], (se,))
            dump("lp", lp_b[0][:, :], (se,))
            dump("fin", xcf[dj][:, :], (ve,))
            dump("hmg", dt_b[dj][:, :], (ve,))
            dump("attn", v_b[dj][:, :], (se,))
            dump("stm", S_tm[0][:, :], (se,))

    kb.step("sync", lambda e: e.nop(), waits=(dq_s,))
    kb.step("gpsimd", lambda e: e.nop(), waits=(dq_g, dq_s))

    import contextlib
    with contextlib.ExitStack() as ctx:
        kb.emit(ctx)
    return nc


_NC_CACHE = []


def kernel(**inputs):
    in_maps = _host_build(inputs)
    if not _NC_CACHE:
        _NC_CACHE.append(_build_program())
    nc = _NC_CACHE[0]
    res = run_bass_kernel_spmd(nc, in_maps, core_ids=list(range(NCORES)))
    y = np.empty((B, N, D), dtype=np.float32)
    for c in range(NCORES):
        seq, half = c // 2, c % 2
        y[seq, half * TPC:(half + 1) * TPC] = res.results[c]["y"]
    return y
